# version 20
# baseline (speedup 1.0000x reference)
import numpy as np
import concourse.bass as bass
import concourse.mybir as mybir
from concourse.bass_utils import run_bass_kernel_spmd
from contextlib import ExitStack

F32 = mybir.dt.float32
BF16 = mybir.dt.bfloat16
FP16 = mybir.dt.float16
FP8 = mybir.dt.float8e4
AF = mybir.ActivationFunctionType
ALU = mybir.AluOpType
DR = mybir.MatmulPerfMode.DoubleRow

D = 1024
KC = 8
TPC = 2048
NT = 4
TT = 512
H = 4096
HC = 32
NCORE = 8
EPS = 1e-5

_CONSTS = [0.0, 1.0, -1.0, 0.5, -0.5, -0.125, 1.0 / 1024.0]

_VEC_NAMES = ["g1", "g2", "blin", "bconv", "ba", "bx", "blin2", "b1", "b2",
              "b3", "lam", "fb2"]


class _Cnt:
    def __init__(self, handle):
        self.h = handle
        self.n = 0


def _build():
    nc = bass.Bass("TRN2", target_bir_lowering=False, debug=False)

    x_in = nc.dram_tensor("x", [D, TPC], F32, kind="ExternalInput").ap()
    w_in = {}
    for nm in ["wlin", "wconv", "wa", "wx", "wlin2", "w1a", "w2a", "w3a"]:
        w_in[nm] = nc.dram_tensor(nm, [D, D], F32, kind="ExternalInput").ap()
    fw1_in = nc.dram_tensor("fw1", [D, H], F32, kind="ExternalInput").ap()
    fw2_in = nc.dram_tensor("fw2", [H, D], F32, kind="ExternalInput").ap()
    v_in = {}
    for nm in _VEC_NAMES:
        v_in[nm] = nc.dram_tensor("v" + nm, [128, KC], F32,
                                  kind="ExternalInput").ap()
    v_in["fb1"] = nc.dram_tensor("vfb1", [128, HC], F32,
                                 kind="ExternalInput").ap()
    v_in["mask"] = nc.dram_tensor("vmask", [128, KC], F32,
                                  kind="ExternalInput").ap()

    x_out = nc.dram_tensor("xout", [D, TPC], F32, kind="ExternalOutput").ap()

    s_ln1 = nc.dram_tensor("s_ln1", [D, TPC], BF16, kind="Internal").ap()
    s_u = nc.dram_tensor("s_u", [D, TPC], BF16, kind="Internal").ap()
    s_v = nc.dram_tensor("s_v", [D, TPC], BF16, kind="Internal").ap()
    s_x1 = nc.dram_tensor("s_x1", [D, TPC], F32, kind="Internal").ap()
    fw2q = nc.dram_tensor("fw2q", [128, HC, D], FP8, kind="Internal").ap()
    fw1q = nc.dram_tensor("fw1q", [128, KC, H], BF16, kind="Internal").ap()
    cc_in = nc.dram_tensor("cc_in", [128, KC], F32, kind="Internal").ap()
    cc_out = nc.dram_tensor("cc_out", [2, 128, KC], F32, kind="Internal").ap()

    xr_in = x_in.rearrange("(c p) t -> p c t", p=128)
    xr_out = x_out.rearrange("(c p) t -> p c t", p=128)
    wr_in = {nm: a.rearrange("(c p) o -> p c o", p=128)
             for nm, a in w_in.items()}
    fw1_r = fw1_in.rearrange("(c p) o -> p c o", p=128)
    fw2_r = fw2_in.rearrange("(c p) o -> p c o", p=128)
    sr_ln1 = s_ln1.rearrange("(c p) t -> p c t", p=128)
    sr_u = s_u.rearrange("(c p) t -> p c t", p=128)
    sr_v = s_v.rearrange("(c p) t -> p c t", p=128)
    sr_x1 = s_x1.rearrange("(c p) t -> p c t", p=128)

    with ExitStack() as top:
        def sb(name, shape, dt_, stack):
            return stack.enter_context(nc.sbuf_tensor(name, shape, dt_))

        cpool = sb("cpool", [128, len(_CONSTS)], F32, top)
        for i, val in enumerate(_CONSTS):
            nc.const_aps.register(F32, val, cpool[:, i:i + 1])
        ones_bf = sb("ones_bf", [128, 1], BF16, top)
        ones_f32 = sb("ones_f32", [128, 1], F32, top)
        ones_fb = sb("ones_fb", [1, 128], BF16, top)
        vec = {nm: sb("sv_" + nm, [128, KC], F32, top) for nm in _VEC_NAMES}
        vec["fb1"] = sb("sv_fb1", [128, HC], F32, top)
        vec["mask"] = sb("sv_mask", [128, KC], F32, top)
        logaC = sb("logaC", [128, KC], F32, top)
        lam_t = sb("lam_t", [128, KC], F32, top)
        bq16 = sb("bq16", [128, KC], FP16, top)
        bq32 = sb("bq32", [128, KC], F32, top)
        carry = sb("carry", [128, KC], F32, top)
        hraw = sb("hraw", [128, KC], F32, top)
        hm = sb("hm", [128, KC], F32, top)
        rows_t = sb("rows_t", [1, 6 * TT], F32, top)
        rows_bf = sb("rows_bf", [1, 3 * TT], BF16, top)
        crows = sb("crows", [1, 2 * NT * TT], BF16, top)

        def row(i):
            return rows_t[0:1, i * TT:(i + 1) * TT]

        def rbf(i):
            return rows_bf[0:1, i * TT:(i + 1) * TT]

        def crow_m(j):
            return crows[0:1, 2 * j * TT:(2 * j + 1) * TT]

        def crow_s(j):
            return crows[0:1, (2 * j + 1) * TT:(2 * j + 2) * TT]

        psS1 = top.enter_context(nc.psum_tensor("psS1", [128, TT], F32))
        psS2 = top.enter_context(nc.psum_tensor("psS2", [128, TT], F32))
        psM0 = top.enter_context(nc.psum_tensor("psM0", [128, TT], F32))
        psM1 = top.enter_context(nc.psum_tensor("psM1", [128, TT], F32))
        psM = [psM0, psM1]

        sems = {nm: _Cnt(top.enter_context(nc.semaphore(nm)))
                for nm in ["dsy", "dgp", "dsc", "sv", "ss", "st", "gc", "cc"]}
        dsy, dgp, dsc, sv, ss, st_, gc, ccs = (
            sems[n] for n in ["dsy", "dgp", "dsc", "sv", "ss", "st", "gc",
                              "cc"])

        block = top.enter_context(nc.Block())
        SY, GP, V, S, T = (block.sync, block.gpsimd, block.vector,
                           block.scalar, block.tensor)
        SC_Q = block.scalar

        def emit(engine_sel, waits, fn, inc=None, inc_k=1):
            def body(eng):
                for cnt, val in waits:
                    if val > 0:
                        eng.wait_ge(cnt.h, val)
                last = fn(eng)
                if inc is not None:
                    assert last is not None
                    last.then_inc(inc.h, inc_k)
            engine_sel(body)
            if inc is not None:
                inc.n += inc_k

        def dma(qsel, out, in_, waits, cnt):
            emit(qsel, waits, lambda e: e.dma_start(out=out, in_=in_), cnt, 16)

        marks = {}

        def rowsC_pre(j):
            def f_rC1(v):
                v.tensor_mul(row(3), crow_m(j), crow_m(j))
                v.tensor_sub(row(4), crow_s(j), row(3))
                v.tensor_scalar_add(row(3), row(4), EPS)
                return v.reciprocal(row(4), row(3))
            w = [(ss, marks[("crows", j)])]
            if j > 0:
                w.append((sv, marks[("rowsC3", j - 1)]))
            emit(V, w, f_rC1, sv)
            marks[("rowsC1", j)] = sv.n

            emit(S, [(sv, marks[("rowsC1", j)])],
                 lambda s: s.activation(row(5), row(4), AF.Sqrt), ss)
            marks[("rowsC2", j)] = ss.n

            def f_rC3(v):
                v.tensor_mul(row(4), crow_m(j), row(5))
                v.tensor_copy(rbf(1), row(5))
                return v.tensor_copy(rbf(2), row(4))
            w = [(ss, marks[("rowsC2", j)])]
            if j > 0:
                w.append((st_, marks[("bcastC", j - 1)]))
            emit(V, w, f_rC3, sv)
            marks[("rowsC3", j)] = sv.n

        for nm in _VEC_NAMES:
            dma(SY, vec[nm][:], v_in[nm], [], dsy)
        dma(SY, vec["fb1"][:], v_in["fb1"], [], dsy)
        dma(SY, vec["mask"][:], v_in["mask"], [], dsy)
        setup_loads = dsy.n

        def setup_v(v):
            for i, val in enumerate(_CONSTS):
                v.memset(cpool[:, i:i + 1], val)
            v.memset(ones_bf[:], 1.0)
            v.memset(ones_f32[:], 1.0)
            v.memset(ones_fb[:], 1.0)
            return v.engine_nop()
        emit(V, [], setup_v, sv)
        consts_ready = sv.n

        def setup_s(s):
            s.wait_ge(sv.h, consts_ready)
            s.activation(lam_t[:], vec["lam"][:], AF.Exp, scale=-1.0)
            s.activation(logaC[:], lam_t[:], AF.Ln, bias=1.0)
            s.activation(logaC[:], logaC[:], AF.Copy, scale=-0.125)
            s.activation(bq16[:], vec["fb2"][:], AF.Copy)
            return s.activation(bq32[:], bq16[:], AF.Copy)
        emit(S, [(dsy, setup_loads)], setup_s, ss)
        setup_s_done = ss.n

        with ExitStack() as wsc:
            wslot = [sb(f"wslot{i}", [128, KC, D], BF16, wsc)
                     for i in range(4)]
            wA = {"wlin": wslot[0], "wconv": wslot[1],
                  "wa": wslot[2], "wx": wslot[3]}
            wB = {"wlin2": wslot[0], "w1a": wslot[1],
                  "w2a": wslot[2], "w3a": wslot[3]}
            for nm in ["wlin", "wconv", "wa", "wx"]:
                dma(GP, wA[nm][:], wr_in[nm], [], dgp)
            wA_done = {nm: 16 * (i + 1) for i, nm in
                       enumerate(["wlin", "wconv", "wa", "wx"])}
            dma(GP, fw2q, fw2_r, [], dgp)
            marks["w2pre"] = dgp.n

            pha = wsc.enter_context(ExitStack())
            psB3 = pha.enter_context(nc.psum_tensor("psB3", [128, 3 * TT],
                                                    F32))
            x0t = sb("x0t", [128, KC, TT], F32, pha)
            lin_bf = sb("lin_bf", [128, KC, TT], BF16, pha)
            ln1b = sb("ln1b", [128, KC, TT], BF16, pha)
            sqA = ln1b
            a_t = sb("a_t", [128, KC, TT + 1], F32, pha)
            xn_t = sb("xn_t", [128, KC, TT], BF16, pha)
            la_t = sb("la_t", [128, KC, TT], BF16, pha)
            xgel_t = sb("xgel_t", [128, KC, TT], BF16, pha)
            xrg_bf = sb("xrg_bf", [128, KC, TT], BF16, pha)
            it_bf = sb("it_bf", [128, KC, TT], BF16, pha)
            gated_t = sb("gated_t", [128, KC, TT + 1], F32, pha)
            tmpc = sb("tmpcA", [128, TT], F32, pha)
            rt_c = [sb(f"rt{i}", [128, TT], BF16, pha) for i in range(2)]
            h_c = sb("h_cA", [128, TT + 1], F32, pha)
            cph = sb("cphA", [128, TT + 1], F32, pha)
            zlead = sb("zlead", [128, TT + 1], F32, pha)
            cpcar = sb("cpcar", [128, KC], F32, pha)

            def f_init(v):
                for c in range(KC):
                    v.memset(a_t[:, c, 0:1], 0.0)
                    v.memset(gated_t[:, c, 0:1], 0.0)
                v.memset(zlead[:], 0.0)
                v.memset(cpcar[:], 1.0)
                return v.engine_nop()
            emit(V, [], f_init, sv)

            def mm8(wsb, rhs, evbase_mark, per, sem, banks):
                def f(t):
                    for odc in range(KC):
                        if odc >= 2:
                            t.wait_ge(sem.h, marks[evbase_mark] +
                                      per * (odc - 1))
                        last = None
                        for kc in range(KC):
                            last = t.matmul(
                                banks[odc % 2][:],
                                wsb[:, kc, odc * 128:(odc + 1) * 128],
                                rhs[:, kc, :],
                                start=(kc == 0), stop=(kc == KC - 1))
                        last.then_inc(st_.h, 1)
                        st_.n += 1
                    return None
                return f

            def fA_load(j):
                xsl = slice(j * TT, (j + 1) * TT)
                w = [] if j == 0 else [(sv, marks[("ln1", j - 1)])]
                dma(SY, x0t[:], xr_in[:, :, xsl], w, dsy)
                marks[("ldx", j)] = dsy.n

                w = [(dsy, marks[("ldx", j)]), (ss, setup_s_done)]
                if j > 0:
                    w += [(st_, marks[("statsA", j - 1)]),
                          (dsy, marks[("dma_ln1", j - 1)])]
                emit(S, w, lambda s: s.activation(sqA[:], x0t[:], AF.Square),
                     ss)
                marks[("sqA", j)] = ss.n

                def f_stats(t):
                    for c in range(KC):
                        t.matmul(psS1[0:1, :], ones_f32[:, 0:1], x0t[:, c, :],
                                 start=(c == 0), stop=(c == KC - 1))
                    last = None
                    for c in range(KC):
                        last = t.matmul(psS2[0:1, :], ones_bf[:, 0:1],
                                        sqA[:, c, :], start=(c == 0),
                                        stop=(c == KC - 1))
                    return last
                w = [(dsy, marks[("ldx", j)]), (ss, marks[("sqA", j)]),
                     (sv, consts_ready)]
                if j > 0:
                    w.append((ss, marks[("rowsA1", j - 1)]))
                emit(T, w, f_stats, st_)
                marks[("statsA", j)] = st_.n

            def fA_rows(j):
                def f_r1(s):
                    s.activation(row(0), psS1[0:1, :], AF.Copy,
                                 scale=1.0 / 1024.0)
                    return s.activation(row(1), psS2[0:1, :], AF.Copy,
                                        scale=1.0 / 1024.0)
                w = [(st_, marks[("statsA", j)])]
                if j > 0:
                    w.append((st_, marks[("bcastA", j - 1)]))
                emit(S, w, f_r1, ss)
                marks[("rowsA1", j)] = ss.n

                def f_r2(v):
                    v.tensor_mul(row(2), row(0), row(0))
                    v.tensor_sub(row(3), row(1), row(2))
                    v.tensor_scalar_max(row(2), row(3), 1e-38)
                    v.tensor_scalar_add(row(1), row(3), EPS)
                    v.reciprocal(row(4), row(2))
                    v.tensor_mul(row(2), row(1), row(4))
                    return v.reciprocal(row(4), row(1))
                emit(V, [(ss, marks[("rowsA1", j)])], f_r2, sv)

                def f_r3(s):
                    s.activation(row(5), row(4), AF.Sqrt)
                    return s.activation(row(2), row(2), AF.Sqrt)
                emit(S, [(sv, sv.n)], f_r3, ss)

                def f_r4(v):
                    v.tensor_mul(row(3), row(0), row(5))
                    v.tensor_copy(rbf(0), row(5))
                    v.tensor_copy(rbf(1), row(3))
                    return v.tensor_copy(rbf(2), row(2))
                emit(V, [(ss, ss.n)], f_r4, sv)
                marks[("rowsA4", j)] = sv.n

            def fA_bc(j):
                xsl = slice(j * TT, (j + 1) * TT)

                def f_bcast(t):
                    t.matmul(psB3[:, 0:TT], ones_fb[0:1, :], rbf(0),
                             start=True, stop=True)
                    t.matmul(psB3[:, TT:2 * TT], ones_fb[0:1, :], rbf(1),
                             start=True, stop=True)
                    return t.matmul(psB3[:, 2 * TT:3 * TT], ones_fb[0:1, :],
                                    rbf(2), start=True, stop=True)
                w = [(sv, marks[("rowsA4", j)])]
                if j > 0:
                    w.append((sv, marks[("ln1", j - 1)]))
                emit(T, w, f_bcast, st_)
                marks[("bcastA", j)] = st_.n

                def f_ln1(v):
                    for c in range(KC):
                        v.tensor_mul(tmpc[:], x0t[:, c, :], psB3[:, 0:TT])
                        v.tensor_sub(ln1b[:, c, :], tmpc[:],
                                     psB3[:, TT:2 * TT])
                        v.scalar_tensor_tensor(xn_t[:, c, :],
                                               ln1b[:, c, :],
                                               vec["g1"][:, c:c + 1],
                                               psB3[:, 2 * TT:3 * TT],
                                               ALU.mult, ALU.mult)
                    return v.engine_nop()
                w = [(st_, marks[("bcastA", j)])]
                if j > 0:
                    w += [(dsy, marks[("dma_ln1", j - 1)]),
                          (st_, marks[("linmm", j - 1)])]
                emit(V, w, f_ln1, sv)
                marks[("ln1", j)] = sv.n

                dma(SY, sr_ln1[:, :, xsl], ln1b[:],
                    [(sv, marks[("ln1", j)])], dsy)
                marks[("dma_ln1", j)] = dsy.n

            def midA1(j):
                marks[("linev_base", j)] = ss.n
                w = [(sv, marks[("ln1", j)]), (dgp, wA_done["wlin"])]
                if j > 0:
                    w.append((ss, marks[("it", j - 1)]))
                emit(T, w, mm8(wA["wlin"], xn_t, ("linev_base", j), 1, ss,
                               psM), None)
                marks[("linmm", j)] = st_.n

                def f_linev(s):
                    base = marks[("linmm", j)] - KC
                    for odc in range(KC):
                        s.wait_ge(st_.h, base + odc + 1)
                        last = s.activation(lin_bf[:, odc, :],
                                            psM[odc % 2][:], AF.Identity,
                                            bias=vec["blin"][:, odc:odc + 1])
                        last.then_inc(ss.h, 1)
                        ss.n += 1
                    return None
                w = []
                if j > 0:
                    w.append((st_, marks[("xrgmm", j - 1)]))
                emit(S, w, f_linev, None)
                marks[("linid", j)] = ss.n

                marks[("xrgev_base", j)] = ss.n
                emit(T, [(ss, marks[("linid", j)]), (dgp, wA_done["wconv"])],
                     mm8(wA["wconv"], lin_bf, ("xrgev_base", j), 1, ss, psM),
                     None)
                marks[("xrgmm", j)] = st_.n

                def f_xrgev(s):
                    base = marks[("xrgmm", j)] - KC
                    for odc in range(KC):
                        s.wait_ge(st_.h, base + odc + 1)
                        last = s.activation(xrg_bf[:, odc, :],
                                            psM[odc % 2][:], AF.Identity,
                                            bias=vec["bconv"][:, odc:odc + 1])
                        last.then_inc(ss.h, 1)
                        ss.n += 1
                    return None
                w = []
                if j > 0:
                    w += [(st_, marks[("itmm", j - 1)]),
                          (sv, marks[("gmul", j - 1)])]
                emit(S, w, f_xrgev, None)
                marks[("xrg", j)] = ss.n

            def midA2(j):
                marks[("rtev_base", j)] = ss.n
                emit(T, [(ss, marks[("xrg", j)]), (dgp, wA_done["wa"])],
                     mm8(wA["wa"], xrg_bf, ("rtev_base", j), 1, ss, psM),
                     None)
                marks[("rtmm", j)] = st_.n

                def f_rtev(s):
                    base = marks[("rtmm", j)] - KC
                    for odc in range(KC):
                        s.wait_ge(st_.h, base + odc + 1)
                        s.activation(rt_c[odc % 2][:], psM[odc % 2][:],
                                     AF.Sigmoid,
                                     bias=vec["ba"][:, odc:odc + 1])
                        last = s.activation(la_t[:, odc, :],
                                            rt_c[odc % 2][:], AF.Copy,
                                            scale=logaC[:, odc:odc + 1])
                        last.then_inc(ss.h, 1)
                        ss.n += 1
                    return None
                w = []
                if j > 0:
                    w.append((dgp, marks[("dma_u", j - 1)]))
                emit(S, w, f_rtev, None)
                marks[("rt", j)] = ss.n

                w = []
                if j > 0:
                    w.append((sv, marks[("scans", j - 1)]))
                emit(S, w, lambda s: s.activation(xgel_t[:], lin_bf[:],
                                                  AF.Gelu), ss)
                marks[("xgel", j)] = ss.n

            def midA3(j):
                marks[("itev_base", j)] = ss.n
                emit(T, [(ss, marks[("rt", j)]), (dgp, wA_done["wx"])],
                     mm8(wA["wx"], xrg_bf, ("itev_base", j), 1, ss, psM),
                     None)
                marks[("itmm", j)] = st_.n

                def f_itev(s):
                    base = marks[("itmm", j)] - KC
                    for odc in range(KC):
                        s.wait_ge(st_.h, base + odc + 1)
                        last = s.activation(it_bf[:, odc, :], psM[odc % 2][:],
                                            AF.Sigmoid,
                                            bias=vec["bx"][:, odc:odc + 1])
                        last.then_inc(ss.h, 1)
                        ss.n += 1
                    return None
                w = []
                if j > 0:
                    w.append((dgp, marks[("dma_v", j - 1)]))
                emit(S, w, f_itev, None)
                marks[("it", j)] = ss.n

            def tailA(j):
                xsl = slice(j * TT, (j + 1) * TT)
                w = []
                if j > 0:
                    w.append((sv, marks[("scans", j - 1)]))
                emit(S, w,
                     lambda s: s.activation(a_t[:, :, 1:TT + 1], la_t[:],
                                            AF.Exp), ss)
                marks[("a", j)] = ss.n

                def f_sg(s):
                    ga = gated_t[:, :, 1:TT + 1]
                    s.activation(ga, a_t[:, :, 1:TT + 1], AF.Square)
                    return s.activation(ga, ga, AF.Sqrt, scale=-1.0,
                                        bias=1.0)
                w = []
                if j > 0:
                    w.append((sv, marks[("hscans", j - 1)]))
                emit(S, w, f_sg, ss)
                marks[("sg", j)] = ss.n

                def f_gmul(v):
                    for c in range(KC):
                        ga = gated_t[:, c, 1:TT + 1]
                        v.tensor_mul(ga, ga, it_bf[:, c, :])
                        v.tensor_mul(ga, ga, xrg_bf[:, c, :])
                    return v.engine_nop()
                w = [(ss, marks[("sg", j)]), (ss, marks[("it", j)])]
                emit(V, w, f_gmul, sv)
                marks[("gmul", j)] = sv.n

                marks[("hs_base", j)] = sv.n

                def f_hloop(v):
                    for c in range(KC):
                        last = v.tensor_tensor_scan(
                            h_c[:], a_t[:, c, 0:TT + 1],
                            gated_t[:, c, 0:TT + 1], 0.0,
                            ALU.mult, ALU.add)
                        last.then_inc(sv.h, 1)
                        sv.n += 1
                        v.wait_ge(sv.h, sv.n)
                        v.tensor_copy(gated_t[:, c, 0:1],
                                      h_c[:, TT:TT + 1])
                        if j == NT - 1:
                            v.tensor_copy(carry[:, c:c + 1],
                                          h_c[:, TT:TT + 1])
                        last = v.tensor_mul(la_t[:, c, :],
                                            h_c[:, 1:TT + 1],
                                            xgel_t[:, c, :])
                        last.then_inc(sv.h, 1)
                        sv.n += 1
                    return None
                w = [(ss, marks[("xgel", j)])]
                if j > 0:
                    w.append((dgp, marks[("dma_u", j - 1)]))
                emit(V, w, f_hloop, None)
                marks[("hscans", j)] = sv.n

                if j == NT - 1:
                    dma(SY, cc_in, carry[:], [(sv, marks[("hscans", j)])],
                        dsy)
                    marks["cc_up"] = dsy.n

                dma(GP, sr_u[:, :, xsl], la_t[:],
                    [(sv, marks[("hscans", j)])], dgp)
                marks[("dma_u", j)] = dgp.n

                if j == NT - 1:
                    emit(GP, [(dsy, marks["cc_up"])],
                         lambda g: g.collective_compute(
                             "AllGather", ALU.bypass,
                             replica_groups=[[0, 1], [2, 3], [4, 5], [6, 7]],
                             ins=[cc_in], outs=[cc_out]), ccs)

                marks[("cp_base", j)] = sv.n

                def f_cploop(v):
                    for c in range(KC):
                        last = v.tensor_copy(zlead[:, 0:1], cpcar[:, c:c + 1])
                        last.then_inc(sv.h, 1)
                        sv.n += 1
                        v.wait_ge(sv.h, sv.n)
                        last = v.tensor_tensor_scan(
                            cph[:], a_t[:, c, 0:TT + 1],
                            zlead[:], 0.0, ALU.mult, ALU.add)
                        last.then_inc(sv.h, 1)
                        sv.n += 1
                        v.wait_ge(sv.h, sv.n)
                        v.tensor_copy(cpcar[:, c:c + 1],
                                      cph[:, TT:TT + 1])
                        last = v.tensor_mul(it_bf[:, c, :],
                                            cph[:, 1:TT + 1],
                                            xgel_t[:, c, :])
                        last.then_inc(sv.h, 1)
                        sv.n += 1
                    return None
                w = []
                if j > 0:
                    w.append((dgp, marks[("dma_v", j - 1)]))
                emit(V, w, f_cploop, None)
                marks[("scans", j)] = sv.n

                dma(GP, sr_v[:, :, xsl], it_bf[:],
                    [(sv, marks[("scans", j)])], dgp)
                marks[("dma_v", j)] = dgp.n

                if j == NT - 1:
                    dma(SY, hraw[:], cc_out[0], [(ccs, 1)], dsy)
                    marks["hraw_ld"] = dsy.n

            fA_load(0)
            fA_rows(0)
            fA_bc(0)
            for j in range(NT):
                if j == 1:
                    for q in range(4):
                        emit(GP, [],
                             lambda g, q=q: g.dma_start(
                                 out=fw1q[:, :, q * 1024:(q + 1) * 1024],
                                 in_=fw1_r[:, :, q * 1024:(q + 1) * 1024]),
                             dgp, 16)
                    marks["w1pre"] = dgp.n
                midA1(j)
                if j == NT - 1:
                    for nm, amark in [("wlin2", ("linmm", j)),
                                      ("w1a", ("xrgmm", j))]:
                        emit(GP, [(st_, marks[amark])],
                             lambda g, nm=nm: g.dma_start(out=wB[nm][:],
                                                          in_=wr_in[nm]),
                             dgp, 16)
                        marks[("wB", nm)] = dgp.n
                if j + 1 < NT:
                    fA_load(j + 1)
                    fA_rows(j + 1)
                midA2(j)
                if j == NT - 1:
                    emit(GP, [(st_, marks[("rtmm", j)])],
                         lambda g: g.dma_start(out=wB["w2a"][:],
                                               in_=wr_in["w2a"]), dgp, 16)
                    marks[("wB", "w2a")] = dgp.n
                if j + 1 < NT:
                    fA_bc(j + 1)
                midA3(j)
                if j == NT - 1:
                    emit(GP, [(st_, marks[("itmm", j)])],
                         lambda g: g.dma_start(out=wB["w3a"][:],
                                               in_=wr_in["w3a"]), dgp, 16)
                    marks[("wB", "w3a")] = dgp.n
                tailA(j)

            a_end = [(sv, sv.n), (ss, ss.n), (st_, st_.n), (gc, gc.n),
                     (dgp, dgp.n), (dsy, dsy.n), (dsc, dsc.n)]
            pha.close()

            phb = wsc.enter_context(ExitStack())
            psB1 = phb.enter_context(nc.psum_tensor("psB1", [128, TT], F32))
            psM2 = phb.enter_context(nc.psum_tensor("psM2", [128, TT], F32))
            psM3 = phb.enter_context(nc.psum_tensor("psM3", [128, TT], F32))
            psL = [psM2, psM3]
            uB = sb("uB", [128, KC, TT], BF16, phb)
            vB = sb("vB", [128, KC, TT], BF16, phb)
            ln1B = sb("ln1B", [128, KC, TT], BF16, phb)
            x0B = sb("x0B", [128, KC, TT], F32, phb)
            xghB = sb("xghB", [128, KC, TT], BF16, phb)
            res2 = sb("res2", [128, KC, TT], F32, phb)
            xmB = sb("xmB", [128, KC, TT], BF16, phb)
            outsB = sb("outsB", [128, KC, TT], BF16, phb)
            gateB = sb("gateB", [128, KC, TT], BF16, phb)
            combB = sb("combB", [128, KC, TT], BF16, phb)
            tvB = sb("tvB", [128, TT], F32, phb)
            x1B = [sb(f"x1B{i}", [128, TT], F32, phb) for i in range(2)]
            x1bf = [sb(f"x1bf{i}", [128, TT], BF16, phb) for i in range(2)]
            x1sq = [sb(f"x1sq{i}", [128, TT], BF16, phb) for i in range(2)]

            def frontB(j):
                xsl = slice(j * TT, (j + 1) * TT)
                w = [(dgp, marks[("dma_u", j)])]
                w += ([(st_, marks[("statsA", NT - 1)]),
                       (sv, marks[("ln1", NT - 1)])] if j == 0 else
                      [(st_, marks[("lin2mm", j - 1)])])
                dma(SY, uB[:], sr_u[:, :, xsl], w, dsy)
                marks[("ld_uB", j)] = dsy.n
                w = [(dgp, marks[("dma_v", j)])]
                w += ([(st_, marks[("statsA", NT - 1)]),
                       (sv, marks[("ln1", NT - 1)])] if j == 0 else
                      [(st_, marks[("lin2mm", j - 1)])])
                dma(SC_Q, vB[:], sr_v[:, :, xsl], w, dsc)
                marks[("ld_vB", j)] = dsc.n
                w = [(dsy, marks[("dma_ln1", j)])]
                w += ([(ss, marks[("xgel", NT - 1)])] if j == 0 else
                      [(sv, marks[("res2", j - 1)])])
                dma(SY, ln1B[:], sr_ln1[:, :, xsl], w, dsy)
                marks[("ld_ln1B", j)] = dsy.n
                def f_xgh(v):
                    for c in range(KC):
                        v.scalar_tensor_tensor(xghB[:, c, :], vB[:, c, :],
                                               hm[:, c:c + 1], uB[:, c, :],
                                               ALU.mult, ALU.add)
                    return v.engine_nop()
                w = [(dsy, marks[("ld_uB", j)]), (dsc, marks[("ld_vB", j)]),
                     (sv, marks["hm"])]
                if j > 0:
                    w += [(st_, marks[("lin2mm", j - 1)]),
                          (st_, marks[("statsB", j - 1)])]
                emit(V, w, f_xgh, sv)
                marks[("xgh", j)] = sv.n

            def loadB_x0(j):
                xsl = slice(j * TT, (j + 1) * TT)
                w = ([(dsy, marks[("dma_ln1", NT - 1)]),
                      (sv, marks[("scans", NT - 1)])] if j == 0 else
                     [(sv, marks[("x1", j - 1)])])
                dma(SC_Q, x0B[:], xr_in[:, :, xsl], w, dsc)
                marks[("ld_x0B", j)] = dsc.n

            def mainB(j):
                marks[("res2_base", j)] = sv.n
                w = [(sv, marks[("xgh", j)]), (dgp, marks[("wB", "wlin2")])]
                if j > 0:
                    w.append((sv, marks[("res2", j - 1)]))
                emit(T, w, mm8(wB["wlin2"], xghB, ("res2_base", j), 1, sv,
                               psL), None)
                marks[("lin2mm", j)] = st_.n

                def f_res2(v):
                    base = marks[("lin2mm", j)] - KC
                    for odc in range(KC):
                        v.wait_ge(st_.h, base + odc + 1)
                        last = v.scalar_tensor_tensor(
                            res2[:, odc, :], psL[odc % 2][:],
                            vec["blin2"][:, odc:odc + 1], ln1B[:, odc, :],
                            ALU.add, ALU.add)
                        last.then_inc(sv.h, 1)
                        sv.n += 1
                    return None
                w = [(dsy, marks[("ld_ln1B", j)])]
                if j > 0:
                    w.append((sv, marks[("xm", j - 1)]))
                else:
                    w.append((dgp, marks[("dma_u", NT - 1)]))
                emit(V, w, f_res2, None)
                marks[("res2", j)] = sv.n

                def f_sqB(s):
                    base = marks[("res2", j)] - KC
                    for odc in range(KC):
                        s.wait_ge(sv.h, base + odc + 1)
                        last = s.activation(xghB[:, odc, :], res2[:, odc, :],
                                            AF.Square)
                        last.then_inc(ss.h, 1)
                        ss.n += 1
                    return None
                emit(S, [(st_, marks[("lin2mm", j)])], f_sqB, None)
                marks[("sqB", j)] = ss.n

                def f_statsB(t):
                    base = marks[("sqB", j)] - KC
                    last = None
                    for c in range(KC):
                        t.wait_ge(ss.h, base + c + 1)
                        last = t.matmul(psS2[0:1, :], ones_bf[:, 0:1],
                                        xghB[:, c, :], start=(c == 0),
                                        stop=(c == KC - 1))
                    return last
                w = []
                if j > 0:
                    w.append((ss, marks[("crows", j - 1)]))
                emit(T, w, f_statsB, st_)
                marks[("statsB", j)] = st_.n

                emit(S, [(st_, marks[("statsB", j)])],
                     lambda s: s.activation(row(0), psS2[0:1, :], AF.Copy),
                     ss)
                marks[("rowsB1", j)] = ss.n

                def f_rB2(v):
                    v.tensor_scalar_max(row(1), row(0), 1e-24)
                    return v.reciprocal(row(2), row(1))
                emit(V, [(ss, marks[("rowsB1", j)])], f_rB2, sv)
                marks[("rowsB2", j)] = sv.n

                emit(S, [(sv, marks[("rowsB2", j)])],
                     lambda s: s.activation(row(2), row(2), AF.Sqrt,
                                            scale=1024.0), ss)
                marks[("rowsB3", j)] = ss.n

                emit(V, [(ss, marks[("rowsB3", j)])],
                     lambda v: v.tensor_copy(rbf(0), row(2)), sv)
                marks[("rowsB4", j)] = sv.n

                w = [(sv, marks[("rowsB4", j)])]
                if j > 0:
                    w.append((sv, marks[("xm", j - 1)]))
                emit(T, w,
                     lambda t: t.matmul(psB1[:, :], ones_fb[0:1, :], rbf(0),
                                        start=True, stop=True), st_)
                marks[("bcastB", j)] = st_.n

                def f_xm(v):
                    for c in range(KC):
                        v.scalar_tensor_tensor(xmB[:, c, :], res2[:, c, :],
                                               vec["g2"][:, c:c + 1],
                                               psB1[:, :],
                                               ALU.mult, ALU.mult)
                    return v.engine_nop()
                w = [(st_, marks[("bcastB", j)])]
                if j > 0:
                    w.append((st_, marks[("w2mm", j - 1)]))
                emit(V, w, f_xm, sv)
                marks[("xm", j)] = sv.n

                marks[("outs_base", j)] = ss.n
                w = [(sv, marks[("xm", j)]), (dgp, marks[("wB", "w1a")])]
                if j > 0:
                    w.append((sv, marks[("x1", j - 1)]))
                emit(T, w, mm8(wB["w1a"], xmB, ("outs_base", j), 1, ss, psM),
                     None)
                marks[("w1mm", j)] = st_.n

                def f_outs(s):
                    base = marks[("w1mm", j)] - KC
                    for odc in range(KC):
                        s.wait_ge(st_.h, base + odc + 1)
                        last = s.activation(outsB[:, odc, :],
                                            psM[odc % 2][:], AF.Sigmoid,
                                            bias=vec["b1"][:, odc:odc + 1])
                        last.then_inc(ss.h, 1)
                        ss.n += 1
                    return None
                emit(S, [], f_outs, None)
                marks[("outs", j)] = ss.n

                w = []
                if j > 0:
                    w.append((sv, marks[("comb", j - 1)]))
                else:
                    w.append((dgp, marks[("dma_v", NT - 1)]))
                emit(S, w,
                     lambda s: s.activation(gateB[:], outsB[:], AF.Gelu), ss)
                marks[("gate", j)] = ss.n

            def tailB(j):
                xsl = slice(j * TT, (j + 1) * TT)
                marks[("comb_base", j)] = sv.n
                emit(T, [(ss, marks[("outs", j)]),
                         (dgp, marks[("wB", "w2a")])],
                     mm8(wB["w2a"], xmB, ("comb_base", j), 1, sv, psM), None)
                marks[("w2mm", j)] = st_.n

                def f_comb(v):
                    base = marks[("w2mm", j)] - KC
                    for odc in range(KC):
                        v.wait_ge(st_.h, base + odc + 1)
                        last = v.scalar_tensor_tensor(
                            combB[:, odc, :], psM[odc % 2][:],
                            vec["b2"][:, odc:odc + 1], gateB[:, odc, :],
                            ALU.add, ALU.mult)
                        last.then_inc(sv.h, 1)
                        sv.n += 1
                    return None
                emit(V, [(ss, marks[("gate", j)])], f_comb, None)
                marks[("comb", j)] = sv.n

                marks[("x1_base", j)] = sv.n
                emit(T, [(sv, marks[("comb", j)]),
                         (dgp, marks[("wB", "w3a")])],
                     mm8(wB["w3a"], combB, ("x1_base", j), 2, sv, psM), None)
                marks[("w3mm", j)] = st_.n

                marks[("x1dma_base", j)] = dsy.n
                marks[("x1bf_base", j)] = ss.n

                def f_x1(v):
                    base = marks[("w3mm", j)] - KC
                    for odc in range(KC):
                        v.wait_ge(st_.h, base + odc + 1)
                        if odc >= 2:
                            v.wait_ge(dsy.h, marks[("x1dma_base", j)] +
                                      16 * (odc - 1))
                            v.wait_ge(ss.h, marks[("x1bf_base", j)] +
                                      odc - 1)
                        v.scalar_tensor_tensor(tvB[:], psM[odc % 2][:],
                                               vec["b3"][:, odc:odc + 1],
                                               res2[:, odc, :],
                                               ALU.add, ALU.add)
                        v.tensor_add(x1B[odc % 2][:], tvB[:],
                                     x0B[:, odc, :])
                        last = v.tensor_mul(x1sq[odc % 2][:],
                                            x1B[odc % 2][:],
                                            x1B[odc % 2][:])
                        last.then_inc(sv.h, 2)
                        sv.n += 2
                    return None
                w = [(dsc, marks[("ld_x0B", j)])]
                if j > 0:
                    w.append((st_, marks[("x1stats", j - 1)]))
                emit(V, w, f_x1, None)
                marks[("x1", j)] = sv.n

                def f_x1bf(s):
                    base = marks[("x1", j)] - 2 * KC
                    for odc in range(KC):
                        s.wait_ge(sv.h, base + 2 * (odc + 1))
                        last = s.activation(x1bf[odc % 2][:],
                                            x1B[odc % 2][:], AF.Copy)
                        last.then_inc(ss.h, 1)
                        ss.n += 1
                    return None
                w = []
                if j > 0:
                    w.append((st_, marks[("x1stats", j - 1)]))
                emit(S, w, f_x1bf, None)
                marks[("x1bf", j)] = ss.n

                def f_x1stats(t):
                    bfb = marks[("x1bf", j)] - KC
                    last = None
                    for odc in range(KC):
                        t.wait_ge(ss.h, bfb + odc + 1)
                        t.matmul(psS1[0:1, :], ones_bf[:, 0:1],
                                 x1bf[odc % 2][:], start=(odc == 0),
                                 stop=(odc == KC - 1))
                        last = t.matmul(psS2[0:1, :], ones_bf[:, 0:1],
                                        x1sq[odc % 2][:], start=(odc == 0),
                                        stop=(odc == KC - 1))
                    return last
                w = [(ss, marks[("rowsB1", j)])]
                emit(T, w, f_x1stats, st_)
                marks[("x1stats", j)] = st_.n

                def f_crows(s):
                    s.activation(crows[0:1, 2 * j * TT:(2 * j + 1) * TT],
                                 psS1[0:1, :], AF.Copy, scale=1.0 / 1024.0)
                    return s.activation(
                        crows[0:1, (2 * j + 1) * TT:(2 * j + 2) * TT],
                        psS2[0:1, :], AF.Copy, scale=1.0 / 1024.0)
                emit(S, [(st_, marks[("x1stats", j)])], f_crows, ss)
                marks[("crows", j)] = ss.n

                def f_x1dma(g):
                    base = marks[("x1", j)] - 2 * KC
                    for odc in range(KC):
                        g.wait_ge(sv.h, base + 2 * (odc + 1))
                        last = g.dma_start(out=sr_x1[:, odc, xsl],
                                           in_=x1B[odc % 2][:])
                        last.then_inc(dsy.h, 16)
                        dsy.n += 16
                    return None
                emit(SY, [], f_x1dma, None)
                marks[("x1dma", j)] = dsy.n

            emit(V, [(dsy, marks["hraw_ld"])],
                 lambda v: v.tensor_mul(hm[:], hraw[:], vec["mask"][:]), sv)
            marks["hm"] = sv.n
            frontB(0)
            loadB_x0(0)
            for j in range(NT):
                mainB(j)
                if j + 1 < NT:
                    frontB(j + 1)
                tailB(j)
                if j == 0:
                    rowsC_pre(0)
                if j + 1 < NT:
                    loadB_x0(j + 1)

            b_end = [(sv, sv.n), (ss, ss.n), (st_, st_.n), (gc, gc.n),
                     (dgp, dgp.n), (dsy, dsy.n), (dsc, dsc.n)]
            phb.close()

        with ExitStack() as phc:
            psB2 = phc.enter_context(nc.psum_tensor("psB2", [128, 2 * TT],
                                                    F32))
            psM2c = phc.enter_context(nc.psum_tensor("psM2c", [128, TT],
                                                     F32))
            psM3c = phc.enter_context(nc.psum_tensor("psM3c", [128, TT],
                                                     F32))
            psW = [psM2c, psM3c]
            x1C = [sb(f"x1C{i}", [128, KC, TT], F32, phc) for i in range(2)]
            w1sb = sb("w1sb", [128, KC, H], BF16, phc)
            w2sb = sb("w2sb", [128, HC, D], FP8, phc)
            region_w = [(sv, marks[("x1", NT - 1)]),
                        (st_, marks[("w3mm", NT - 1)]),
                        (ss, marks[("gate", NT - 1)]),
                        (dgp, marks["w1pre"])]
            dma(SY, w1sb[:, :, 0:1024], fw1q[:, :, 0:1024], region_w, dsy)
            marks[("w1q", 0)] = dsy.n
            dma(SY, w2sb[:], fw2q,
                [(st_, marks[("x1stats", NT - 1)]), (dgp, marks["w2pre"])],
                dsy)
            w2_done = dsy.n
            for q in (1, 3):
                dma(SY, w1sb[:, :, q * 1024:(q + 1) * 1024],
                    fw1q[:, :, q * 1024:(q + 1) * 1024], [], dsy)
                marks[("w1q", q)] = dsy.n

            ln2C = [sb(f"ln2C{i}", [128, KC, TT], BF16, phc)
                    for i in range(2)]
            hqC = sb("hqC", [128, HC, TT], FP8, phc)
            o16 = [sb(f"o16_{i}", [128, TT], FP16, phc) for i in range(2)]
            xoC = [sb(f"xoC{i}", [128, TT], F32, phc) for i in range(2)]
            tmpC = sb("tmpC", [128, TT], F32, phc)

            def frontC(j):
                xsl = slice(j * TT, (j + 1) * TT)
                x1t = x1C[j % 2]
                w = [(dsy, marks[("x1dma", j)])]
                if j == 0:
                    w.append((sv, marks[("xgh", NT - 1)]))
                if j == 1:
                    w.append((sv, marks[("x1", NT - 1)]))
                if j > 1:
                    w.append((sv, marks[("xoutV", j - 2)]))
                dma(SC_Q, x1t[:], sr_x1[:, :, xsl], w, dsc)
                marks[("ld_x1C", j)] = dsc.n

                if j > 0:
                    rowsC_pre(j)

                def f_bcastC(t):
                    t.matmul(psB2[:, 0:TT], ones_fb[0:1, :], rbf(1),
                             start=True, stop=True)
                    return t.matmul(psB2[:, TT:2 * TT], ones_fb[0:1, :],
                                    rbf(2), start=True, stop=True)
                w = [(sv, marks[("rowsC3", j)])]
                if j > 0:
                    w.append((sv, marks[("ln2", j - 1)]))
                emit(T, w, f_bcastC, st_)
                marks[("bcastC", j)] = st_.n

                def f_ln2(v):
                    for c in range(KC):
                        v.tensor_mul(tmpC[:], x1t[:, c, :], psB2[:, 0:TT])
                        v.tensor_sub(ln2C[j % 2][:, c, :], tmpC[:],
                                     psB2[:, TT:2 * TT])
                    return v.engine_nop()
                w = [(st_, marks[("bcastC", j)]),
                     (dsc, marks[("ld_x1C", j)])]
                if j > 1:
                    w.append((st_, marks[("w1mmC", j - 2)]))
                emit(V, w, f_ln2, sv)
                marks[("ln2", j)] = sv.n

            def mainC(j):
                marks[("hq_base", j)] = ss.n

                def f_w1(t):
                    for hc in range(HC):
                        if hc >= 2:
                            t.wait_ge(ss.h, marks[("hq_base", j)] + hc - 1)
                        if hc == 8:
                            t.wait_ge(dsy.h, marks[("w1q", 1)])
                        if hc == 16:
                            t.wait_ge(dsc.h, marks[("w1q", 2)])
                        if hc == 24:
                            t.wait_ge(dsy.h, marks[("w1q", 3)])
                        last = None
                        for kc in range(KC):
                            last = t.matmul(
                                psM[hc % 2][:],
                                w1sb[:, kc, hc * 128:(hc + 1) * 128],
                                ln2C[j % 2][:, kc, :],
                                start=(kc == 0), stop=(kc == KC - 1))
                        last.then_inc(st_.h, 1)
                        st_.n += 1
                    return None
                w = [(sv, marks[("ln2", j)]), (dsy, marks[("w1q", 0)])]
                if j > 0:
                    w.append((ss, marks[("hq", j - 1)]))
                emit(T, w, f_w1, None)
                marks[("w1mmC", j)] = st_.n

                def f_hq(s):
                    base = marks[("w1mmC", j)] - HC
                    for hc in range(HC):
                        s.wait_ge(st_.h, base + hc + 1)
                        last = s.activation(hqC[:, hc, :], psM[hc % 2][:],
                                            AF.Relu,
                                            bias=vec["fb1"][:, hc:hc + 1])
                        last.then_inc(ss.h, 1)
                        ss.n += 1
                    return None
                w = []
                if j > 0:
                    w.append((st_, marks[("w2mmC", j - 1)]))
                emit(S, w, f_hq, None)
                marks[("hq", j)] = ss.n

            def tailC(j):
                xsl = slice(j * TT, (j + 1) * TT)
                x1t = x1C[j % 2]
                marks[("o16_base", j)] = ss.n

                def f_w2(t):
                    for odc in range(KC):
                        if odc >= 2:
                            t.wait_ge(ss.h, marks[("o16_base", j)] + odc - 1)
                        last = None
                        for ck in range(HC // 2):
                            last = t.matmul(
                                psW[odc % 2][:],
                                w2sb[:, 2 * ck:2 * ck + 2,
                                     odc * 128:(odc + 1) * 128],
                                hqC[:, 2 * ck:2 * ck + 2, :],
                                start=(ck == 0), stop=(ck == HC // 2 - 1),
                                perf_mode=DR)
                        last.then_inc(st_.h, 1)
                        st_.n += 1
                    return None
                w = [(ss, marks[("hq", j)]), (dsy, w2_done)]
                if j > 0:
                    w.append((ss, marks[("o16", j - 1)]))
                emit(T, w, f_w2, None)
                marks[("w2mmC", j)] = st_.n

                marks[("xoutV_base", j)] = sv.n

                def f_o16(s):
                    base = marks[("w2mmC", j)] - KC
                    for odc in range(KC):
                        s.wait_ge(st_.h, base + odc + 1)
                        if odc >= 2:
                            s.wait_ge(sv.h, marks[("xoutV_base", j)] +
                                      odc - 1)
                        last = s.activation(o16[odc % 2][:], psW[odc % 2][:],
                                            AF.Identity,
                                            bias=bq32[:, odc:odc + 1])
                        last.then_inc(ss.h, 1)
                        ss.n += 1
                    return None
                emit(S, [], f_o16, None)
                marks[("o16", j)] = ss.n

                marks[("xodma_base", j)] = dsy.n

                def f_xout(v):
                    base = marks[("o16", j)] - KC
                    for odc in range(KC):
                        v.wait_ge(ss.h, base + odc + 1)
                        if odc >= 2:
                            v.wait_ge(dsy.h, marks[("xodma_base", j)] +
                                      16 * (odc - 1))
                        last = v.tensor_add(xoC[odc % 2][:], o16[odc % 2][:],
                                            x1t[:, odc, :])
                        last.then_inc(sv.h, 1)
                        sv.n += 1
                    return None
                emit(V, [], f_xout, None)
                marks[("xoutV", j)] = sv.n

                def f_xodma(sync):
                    base = marks[("xoutV", j)] - KC
                    for odc in range(KC):
                        sync.wait_ge(sv.h, base + odc + 1)
                        last = sync.dma_start(out=xr_out[:, odc, xsl],
                                              in_=xoC[odc % 2][:])
                        last.then_inc(dsy.h, 16)
                        dsy.n += 16
                    return None
                emit(SY, [], f_xodma, None)
                marks[("xodma", j)] = dsy.n

            frontC(0)
            frontC(1)
            dma(SC_Q, w1sb[:, :, 2 * 1024:3 * 1024],
                fw1q[:, :, 2 * 1024:3 * 1024],
                [(st_, marks[("w3mm", NT - 1)]),
                 (ss, marks[("gate", NT - 1)]), (dgp, marks["w1pre"])], dsc)
            marks[("w1q", 2)] = dsc.n
            for j in range(NT):
                mainC(j)
                tailC(j)
                if j + 2 < NT:
                    frontC(j + 2)

            emit(SY, [(dsy, dsy.n)], lambda e: None, None)

    return nc


_CACHE = {}


def _get_prog():
    if "nc" not in _CACHE:
        _CACHE["nc"] = _build()
    return _CACHE["nc"]


def kernel(**inputs):
    inp = {k: np.asarray(v) for k, v in inputs.items()}

    def vstage(v, kc):
        v = np.asarray(v, np.float32)
        return np.ascontiguousarray(v.reshape(kc, 128).T)

    def wt(name):
        return np.ascontiguousarray(np.asarray(inp[name], np.float32).T)

    shared = {
        "wlin": wt("L0_attn_W_lin"), "wconv": wt("L0_attn_W_conv"),
        "wa": wt("L0_attn_Wa"), "wx": wt("L0_attn_Wx"),
        "wlin2": wt("L0_attn_W_lin2"), "w1a": wt("L0_attn_W1"),
        "w2a": wt("L0_attn_W2"), "w3a": wt("L0_attn_W3"),
        "fw1": wt("L0_ffn_w1"), "fw2": wt("L0_ffn_w2"),
        "vg1": vstage(inp["L0_attn_g1"], KC),
        "vg2": vstage(inp["L0_attn_g2"], KC),
        "vblin": vstage(inp["L0_attn_b_lin"], KC),
        "vbconv": vstage(inp["L0_attn_b_conv"], KC),
        "vba": vstage(inp["L0_attn_ba"], KC),
        "vbx": vstage(inp["L0_attn_bx"], KC),
        "vblin2": vstage(inp["L0_attn_b_lin2"], KC),
        "vb1": vstage(inp["L0_attn_b1"], KC),
        "vb2": vstage(inp["L0_attn_b2"], KC),
        "vb3": vstage(inp["L0_attn_b3"], KC),
        "vlam": vstage(inp["L0_attn_Lam"], KC),
        "vfb2": vstage(inp["L0_ffn_b2"], KC),
        "vfb1": vstage(inp["L0_ffn_b1"], HC),
    }

    x = np.asarray(inp["L0_attn_x"], np.float32)
    in_maps = []
    for c in range(NCORE):
        b, half = c // 2, c % 2
        t0 = half * TPC
        m = dict(shared)
        m["x"] = np.ascontiguousarray(x[b, t0:t0 + TPC, :].T)
        m["vmask"] = np.full((128, KC), float(half), np.float32)
        in_maps.append(m)

    nc = _get_prog()
    res = run_bass_kernel_spmd(nc, in_maps, core_ids=list(range(NCORE)))

    out = np.empty((4, 4096, D), np.float32)
    for c in range(NCORE):
        b, half = c // 2, c % 2
        t0 = half * TPC
        out[b, t0:t0 + TPC, :] = res.results[c]["xout"].T
    return out


# version 21
# speedup vs baseline: 1.0442x; 1.0442x over previous
import numpy as np
import concourse.bass as bass
import concourse.mybir as mybir
from concourse.bass_utils import run_bass_kernel_spmd
from contextlib import ExitStack

F32 = mybir.dt.float32
BF16 = mybir.dt.bfloat16
FP16 = mybir.dt.float16
FP8 = mybir.dt.float8e4
AF = mybir.ActivationFunctionType
ALU = mybir.AluOpType
DR = mybir.MatmulPerfMode.DoubleRow

D = 1024
KC = 8
TPC = 2048
NT = 4
TT = 512
H = 4096
HC = 32
NCORE = 8
EPS = 1e-5

_CONSTS = [0.0, 1.0, -1.0, 0.5, -0.5, -0.125, 1.0 / 1024.0]

_VEC_NAMES = ["g1", "g2", "blin", "bconv", "ba", "bx", "blin2", "b1", "b2",
              "b3", "lam", "fb2"]


class _Cnt:
    def __init__(self, handle):
        self.h = handle
        self.n = 0


def _build():
    nc = bass.Bass("TRN2", target_bir_lowering=False, debug=False)

    x_in = nc.dram_tensor("x", [D, TPC], F32, kind="ExternalInput").ap()
    w_in = {}
    for nm in ["wlin", "wconv", "wa", "wx", "wlin2", "w1a", "w2a", "w3a"]:
        w_in[nm] = nc.dram_tensor(nm, [D, D], F32, kind="ExternalInput").ap()
    fw1_in = nc.dram_tensor("fw1", [D, H], F32, kind="ExternalInput").ap()
    fw2_in = nc.dram_tensor("fw2", [H, D], F32, kind="ExternalInput").ap()
    v_in = {}
    for nm in _VEC_NAMES:
        v_in[nm] = nc.dram_tensor("v" + nm, [128, KC], F32,
                                  kind="ExternalInput").ap()
    v_in["fb1"] = nc.dram_tensor("vfb1", [128, HC], F32,
                                 kind="ExternalInput").ap()
    v_in["mask"] = nc.dram_tensor("vmask", [128, KC], F32,
                                  kind="ExternalInput").ap()

    x_out = nc.dram_tensor("xout", [D, TPC], F32, kind="ExternalOutput").ap()

    s_ln1 = nc.dram_tensor("s_ln1", [D, TPC], BF16, kind="Internal").ap()
    s_u = nc.dram_tensor("s_u", [D, TPC], BF16, kind="Internal").ap()
    s_v = nc.dram_tensor("s_v", [D, TPC], BF16, kind="Internal").ap()
    s_x1 = nc.dram_tensor("s_x1", [D, TPC], F32, kind="Internal").ap()
    fw2q = nc.dram_tensor("fw2q", [128, HC, D], FP8, kind="Internal").ap()
    fw1q = nc.dram_tensor("fw1q", [128, KC, H], BF16, kind="Internal").ap()
    cc_in = nc.dram_tensor("cc_in", [128, KC], F32, kind="Internal").ap()
    cc_out = nc.dram_tensor("cc_out", [2, 128, KC], F32, kind="Internal").ap()

    xr_in = x_in.rearrange("(c p) t -> p c t", p=128)
    xr_out = x_out.rearrange("(c p) t -> p c t", p=128)
    wr_in = {nm: a.rearrange("(c p) o -> p c o", p=128)
             for nm, a in w_in.items()}
    fw1_r = fw1_in.rearrange("(c p) o -> p c o", p=128)
    fw2_r = fw2_in.rearrange("(c p) o -> p c o", p=128)
    sr_ln1 = s_ln1.rearrange("(c p) t -> p c t", p=128)
    sr_u = s_u.rearrange("(c p) t -> p c t", p=128)
    sr_v = s_v.rearrange("(c p) t -> p c t", p=128)
    sr_x1 = s_x1.rearrange("(c p) t -> p c t", p=128)

    with ExitStack() as top:
        def sb(name, shape, dt_, stack):
            return stack.enter_context(nc.sbuf_tensor(name, shape, dt_))

        cpool = sb("cpool", [128, len(_CONSTS)], F32, top)
        for i, val in enumerate(_CONSTS):
            nc.const_aps.register(F32, val, cpool[:, i:i + 1])
        ones_bf = sb("ones_bf", [128, 1], BF16, top)
        ones_f32 = sb("ones_f32", [128, 1], F32, top)
        ones_fb = sb("ones_fb", [1, 128], BF16, top)
        vec = {nm: sb("sv_" + nm, [128, KC], F32, top) for nm in _VEC_NAMES}
        vec["fb1"] = sb("sv_fb1", [128, HC], F32, top)
        vec["mask"] = sb("sv_mask", [128, KC], F32, top)
        logaC = sb("logaC", [128, KC], F32, top)
        lam_t = sb("lam_t", [128, KC], F32, top)
        bq16 = sb("bq16", [128, KC], FP16, top)
        bq32 = sb("bq32", [128, KC], F32, top)
        carry = sb("carry", [128, KC], F32, top)
        hraw = sb("hraw", [128, KC], F32, top)
        hm = sb("hm", [128, KC], F32, top)
        rows_t = sb("rows_t", [1, 6 * TT], F32, top)
        rows_bf = sb("rows_bf", [1, 3 * TT], BF16, top)
        crows = sb("crows", [1, 2 * NT * TT], BF16, top)

        def row(i):
            return rows_t[0:1, i * TT:(i + 1) * TT]

        def rbf(i):
            return rows_bf[0:1, i * TT:(i + 1) * TT]

        def crow_m(j):
            return crows[0:1, 2 * j * TT:(2 * j + 1) * TT]

        def crow_s(j):
            return crows[0:1, (2 * j + 1) * TT:(2 * j + 2) * TT]

        psS1 = top.enter_context(nc.psum_tensor("psS1", [128, TT], F32))
        psS2 = top.enter_context(nc.psum_tensor("psS2", [128, TT], F32))
        psM0 = top.enter_context(nc.psum_tensor("psM0", [128, TT], F32))
        psM1 = top.enter_context(nc.psum_tensor("psM1", [128, TT], F32))
        psM = [psM0, psM1]

        sems = {nm: _Cnt(top.enter_context(nc.semaphore(nm)))
                for nm in ["dsy", "dgp", "dsc", "sv", "ss", "st", "gc", "cc"]}
        dsy, dgp, dsc, sv, ss, st_, gc, ccs = (
            sems[n] for n in ["dsy", "dgp", "dsc", "sv", "ss", "st", "gc",
                              "cc"])

        block = top.enter_context(nc.Block())
        SY, GP, V, S, T = (block.sync, block.gpsimd, block.vector,
                           block.scalar, block.tensor)
        SC_Q = block.scalar

        def emit(engine_sel, waits, fn, inc=None, inc_k=1):
            def body(eng):
                for cnt, val in waits:
                    if val > 0:
                        eng.wait_ge(cnt.h, val)
                last = fn(eng)
                if inc is not None:
                    assert last is not None
                    last.then_inc(inc.h, inc_k)
            engine_sel(body)
            if inc is not None:
                inc.n += inc_k

        def dma(qsel, out, in_, waits, cnt):
            emit(qsel, waits, lambda e: e.dma_start(out=out, in_=in_), cnt, 16)

        marks = {}

        def rowsC_pre(j):
            def f_rC1(v):
                v.tensor_mul(row(3), crow_m(j), crow_m(j))
                v.tensor_sub(row(4), crow_s(j), row(3))
                v.tensor_scalar_add(row(3), row(4), EPS)
                return v.reciprocal(row(4), row(3))
            w = [(ss, marks[("crows", j)])]
            if j > 0:
                w.append((sv, marks[("rowsC3", j - 1)]))
            emit(V, w, f_rC1, sv)
            marks[("rowsC1", j)] = sv.n

            emit(S, [(sv, marks[("rowsC1", j)])],
                 lambda s: s.activation(row(5), row(4), AF.Sqrt), ss)
            marks[("rowsC2", j)] = ss.n

            def f_rC3(v):
                v.tensor_mul(row(4), crow_m(j), row(5))
                v.tensor_copy(rbf(1), row(5))
                return v.tensor_copy(rbf(2), row(4))
            w = [(ss, marks[("rowsC2", j)])]
            if j > 0:
                w.append((st_, marks[("bcastC", j - 1)]))
            emit(V, w, f_rC3, sv)
            marks[("rowsC3", j)] = sv.n

        for nm in _VEC_NAMES:
            dma(SY, vec[nm][:], v_in[nm], [], dsy)
        dma(SY, vec["fb1"][:], v_in["fb1"], [], dsy)
        dma(SY, vec["mask"][:], v_in["mask"], [], dsy)
        setup_loads = dsy.n

        def setup_v(v):
            for i, val in enumerate(_CONSTS):
                v.memset(cpool[:, i:i + 1], val)
            v.memset(ones_bf[:], 1.0)
            v.memset(ones_f32[:], 1.0)
            v.memset(ones_fb[:], 1.0)
            return v.engine_nop()
        emit(V, [], setup_v, sv)
        consts_ready = sv.n

        def setup_s(s):
            s.wait_ge(sv.h, consts_ready)
            s.activation(lam_t[:], vec["lam"][:], AF.Exp, scale=-1.0)
            s.activation(logaC[:], lam_t[:], AF.Ln, bias=1.0)
            s.activation(logaC[:], logaC[:], AF.Copy, scale=-0.125)
            s.activation(bq16[:], vec["fb2"][:], AF.Copy)
            return s.activation(bq32[:], bq16[:], AF.Copy)
        emit(S, [(dsy, setup_loads)], setup_s, ss)
        setup_s_done = ss.n

        with ExitStack() as wsc:
            wslot = [sb(f"wslot{i}", [128, KC, D], BF16, wsc)
                     for i in range(4)]
            wA = {"wlin": wslot[0], "wconv": wslot[1],
                  "wa": wslot[2], "wx": wslot[3]}
            wB = {"wlin2": wslot[0], "w1a": wslot[1],
                  "w2a": wslot[2], "w3a": wslot[3]}
            for nm in ["wlin", "wconv", "wa", "wx"]:
                dma(GP, wA[nm][:], wr_in[nm], [], dgp)
            wA_done = {nm: 16 * (i + 1) for i, nm in
                       enumerate(["wlin", "wconv", "wa", "wx"])}
            dma(GP, fw2q, fw2_r, [], dgp)
            marks["w2pre"] = dgp.n

            pha = wsc.enter_context(ExitStack())
            psB3 = pha.enter_context(nc.psum_tensor("psB3", [128, 3 * TT],
                                                    F32))
            x0t = sb("x0t", [128, KC, TT], F32, pha)
            lin_bf = sb("lin_bf", [128, KC, TT], BF16, pha)
            ln1b = sb("ln1b", [128, KC, TT], BF16, pha)
            sqA = ln1b
            a_t = sb("a_t", [128, KC, TT + 1], F32, pha)
            xn_t = sb("xn_t", [128, KC, TT], BF16, pha)
            la_t = sb("la_t", [128, KC, TT], BF16, pha)
            xgel_t = sb("xgel_t", [128, KC, TT], BF16, pha)
            xrg_bf = sb("xrg_bf", [128, KC, TT], BF16, pha)
            it_bf = sb("it_bf", [128, KC, TT], BF16, pha)
            gated_t = sb("gated_t", [128, KC, TT + 1], F32, pha)
            tmpc = sb("tmpcA", [128, TT], F32, pha)
            rt_c = [sb(f"rt{i}", [128, TT], BF16, pha) for i in range(2)]
            h_c = sb("h_cA", [128, TT + 1], F32, pha)
            cph = sb("cphA", [128, TT + 1], F32, pha)
            zlead = sb("zlead", [128, TT + 1], F32, pha)
            cpcar = sb("cpcar", [128, KC], F32, pha)

            def f_init(v):
                for c in range(KC):
                    v.memset(a_t[:, c, 0:1], 0.0)
                    v.memset(gated_t[:, c, 0:1], 0.0)
                v.memset(zlead[:], 0.0)
                v.memset(cpcar[:], 1.0)
                return v.engine_nop()
            emit(V, [], f_init, sv)

            def mm8(wsb, rhs, evbase_mark, per, sem, banks):
                def f(t):
                    for odc in range(KC):
                        if odc >= 2:
                            t.wait_ge(sem.h, marks[evbase_mark] +
                                      per * (odc - 1))
                        last = None
                        for kc in range(KC):
                            last = t.matmul(
                                banks[odc % 2][:],
                                wsb[:, kc, odc * 128:(odc + 1) * 128],
                                rhs[:, kc, :],
                                start=(kc == 0), stop=(kc == KC - 1))
                        last.then_inc(st_.h, 1)
                        st_.n += 1
                    return None
                return f

            def fA_load(j):
                xsl = slice(j * TT, (j + 1) * TT)
                w = [] if j == 0 else [(sv, marks[("ln1", j - 1)])]
                dma(SY, x0t[:], xr_in[:, :, xsl], w, dsy)
                marks[("ldx", j)] = dsy.n

                w = [(dsy, marks[("ldx", j)]), (ss, setup_s_done)]
                if j > 0:
                    w += [(st_, marks[("statsA", j - 1)]),
                          (dsy, marks[("dma_ln1", j - 1)])]
                emit(S, w, lambda s: s.activation(sqA[:], x0t[:], AF.Square),
                     ss)
                marks[("sqA", j)] = ss.n

                def f_stats(t):
                    for c in range(KC):
                        t.matmul(psS1[0:1, :], ones_f32[:, 0:1], x0t[:, c, :],
                                 start=(c == 0), stop=(c == KC - 1))
                    last = None
                    for c in range(KC):
                        last = t.matmul(psS2[0:1, :], ones_bf[:, 0:1],
                                        sqA[:, c, :], start=(c == 0),
                                        stop=(c == KC - 1))
                    return last
                w = [(dsy, marks[("ldx", j)]), (ss, marks[("sqA", j)]),
                     (sv, consts_ready)]
                if j > 0:
                    w.append((ss, marks[("rowsA1", j - 1)]))
                emit(T, w, f_stats, st_)
                marks[("statsA", j)] = st_.n

            def fA_rows(j):
                def f_r1(s):
                    s.activation(row(0), psS1[0:1, :], AF.Copy,
                                 scale=1.0 / 1024.0)
                    return s.activation(row(1), psS2[0:1, :], AF.Copy,
                                        scale=1.0 / 1024.0)
                w = [(st_, marks[("statsA", j)])]
                if j > 0:
                    w.append((st_, marks[("bcastA", j - 1)]))
                emit(S, w, f_r1, ss)
                marks[("rowsA1", j)] = ss.n

                def f_r2(v):
                    v.tensor_mul(row(2), row(0), row(0))
                    v.tensor_sub(row(3), row(1), row(2))
                    v.tensor_scalar_max(row(2), row(3), 1e-38)
                    v.tensor_scalar_add(row(1), row(3), EPS)
                    v.reciprocal(row(4), row(2))
                    v.tensor_mul(row(2), row(1), row(4))
                    return v.reciprocal(row(4), row(1))
                emit(V, [(ss, marks[("rowsA1", j)])], f_r2, sv)

                def f_r3(s):
                    s.activation(row(5), row(4), AF.Sqrt)
                    return s.activation(row(2), row(2), AF.Sqrt)
                emit(S, [(sv, sv.n)], f_r3, ss)

                def f_r4(v):
                    v.tensor_mul(row(3), row(0), row(5))
                    v.tensor_copy(rbf(0), row(5))
                    v.tensor_copy(rbf(1), row(3))
                    return v.tensor_copy(rbf(2), row(2))
                emit(V, [(ss, ss.n)], f_r4, sv)
                marks[("rowsA4", j)] = sv.n

            def fA_bc(j):
                xsl = slice(j * TT, (j + 1) * TT)

                def f_bcast(t):
                    t.matmul(psB3[:, 0:TT], ones_fb[0:1, :], rbf(0),
                             start=True, stop=True)
                    t.matmul(psB3[:, TT:2 * TT], ones_fb[0:1, :], rbf(1),
                             start=True, stop=True)
                    return t.matmul(psB3[:, 2 * TT:3 * TT], ones_fb[0:1, :],
                                    rbf(2), start=True, stop=True)
                w = [(sv, marks[("rowsA4", j)])]
                if j > 0:
                    w.append((sv, marks[("ln1", j - 1)]))
                emit(T, w, f_bcast, st_)
                marks[("bcastA", j)] = st_.n

                def f_ln1(v):
                    for c in range(KC):
                        v.tensor_mul(tmpc[:], x0t[:, c, :], psB3[:, 0:TT])
                        v.tensor_sub(ln1b[:, c, :], tmpc[:],
                                     psB3[:, TT:2 * TT])
                        v.scalar_tensor_tensor(xn_t[:, c, :],
                                               ln1b[:, c, :],
                                               vec["g1"][:, c:c + 1],
                                               psB3[:, 2 * TT:3 * TT],
                                               ALU.mult, ALU.mult)
                    return v.engine_nop()
                w = [(st_, marks[("bcastA", j)])]
                if j > 0:
                    w += [(dsy, marks[("dma_ln1", j - 1)]),
                          (st_, marks[("linmm", j - 1)])]
                emit(V, w, f_ln1, sv)
                marks[("ln1", j)] = sv.n

                dma(SY, sr_ln1[:, :, xsl], ln1b[:],
                    [(sv, marks[("ln1", j)])], dsy)
                marks[("dma_ln1", j)] = dsy.n

            def midA1(j):
                marks[("linev_base", j)] = ss.n
                w = [(sv, marks[("ln1", j)]), (dgp, wA_done["wlin"])]
                if j > 0:
                    w.append((ss, marks[("it", j - 1)]))
                emit(T, w, mm8(wA["wlin"], xn_t, ("linev_base", j), 1, ss,
                               psM), None)
                marks[("linmm", j)] = st_.n

                def f_linev(s):
                    base = marks[("linmm", j)] - KC
                    for odc in range(KC):
                        s.wait_ge(st_.h, base + odc + 1)
                        last = s.activation(lin_bf[:, odc, :],
                                            psM[odc % 2][:], AF.Identity,
                                            bias=vec["blin"][:, odc:odc + 1])
                        last.then_inc(ss.h, 1)
                        ss.n += 1
                    return None
                w = []
                if j > 0:
                    w.append((st_, marks[("xrgmm", j - 1)]))
                emit(S, w, f_linev, None)
                marks[("linid", j)] = ss.n

                marks[("xrgev_base", j)] = ss.n
                emit(T, [(ss, marks[("linid", j)]), (dgp, wA_done["wconv"])],
                     mm8(wA["wconv"], lin_bf, ("xrgev_base", j), 1, ss, psM),
                     None)
                marks[("xrgmm", j)] = st_.n

                def f_xrgev(s):
                    base = marks[("xrgmm", j)] - KC
                    for odc in range(KC):
                        s.wait_ge(st_.h, base + odc + 1)
                        last = s.activation(xrg_bf[:, odc, :],
                                            psM[odc % 2][:], AF.Identity,
                                            bias=vec["bconv"][:, odc:odc + 1])
                        last.then_inc(ss.h, 1)
                        ss.n += 1
                    return None
                w = []
                if j > 0:
                    w += [(st_, marks[("itmm", j - 1)]),
                          (sv, marks[("gmul", j - 1)])]
                emit(S, w, f_xrgev, None)
                marks[("xrg", j)] = ss.n

            def midA2(j):
                marks[("rtev_base", j)] = ss.n
                emit(T, [(ss, marks[("xrg", j)]), (dgp, wA_done["wa"])],
                     mm8(wA["wa"], xrg_bf, ("rtev_base", j), 1, ss, psM),
                     None)
                marks[("rtmm", j)] = st_.n

                def f_rtev(s):
                    base = marks[("rtmm", j)] - KC
                    for odc in range(KC):
                        s.wait_ge(st_.h, base + odc + 1)
                        s.activation(rt_c[odc % 2][:], psM[odc % 2][:],
                                     AF.Sigmoid,
                                     bias=vec["ba"][:, odc:odc + 1])
                        last = s.activation(la_t[:, odc, :],
                                            rt_c[odc % 2][:], AF.Copy,
                                            scale=logaC[:, odc:odc + 1])
                        last.then_inc(ss.h, 1)
                        ss.n += 1
                    return None
                w = []
                if j > 0:
                    w.append((dgp, marks[("dma_u", j - 1)]))
                emit(S, w, f_rtev, None)
                marks[("rt", j)] = ss.n

                w = []
                if j > 0:
                    w.append((sv, marks[("scans", j - 1)]))
                emit(S, w, lambda s: s.activation(xgel_t[:], lin_bf[:],
                                                  AF.Gelu), ss)
                marks[("xgel", j)] = ss.n

            def midA3(j):
                marks[("itev_base", j)] = ss.n
                emit(T, [(ss, marks[("rt", j)]), (dgp, wA_done["wx"])],
                     mm8(wA["wx"], xrg_bf, ("itev_base", j), 1, ss, psM),
                     None)
                marks[("itmm", j)] = st_.n

                def f_itev(s):
                    base = marks[("itmm", j)] - KC
                    for odc in range(KC):
                        s.wait_ge(st_.h, base + odc + 1)
                        last = s.activation(it_bf[:, odc, :], psM[odc % 2][:],
                                            AF.Sigmoid,
                                            bias=vec["bx"][:, odc:odc + 1])
                        last.then_inc(ss.h, 1)
                        ss.n += 1
                    return None
                w = []
                if j > 0:
                    w.append((dgp, marks[("dma_v", j - 1)]))
                emit(S, w, f_itev, None)
                marks[("it", j)] = ss.n

            def tailA(j):
                xsl = slice(j * TT, (j + 1) * TT)
                w = []
                if j > 0:
                    w.append((sv, marks[("scans", j - 1)]))
                emit(S, w,
                     lambda s: s.activation(a_t[:, :, 1:TT + 1], la_t[:],
                                            AF.Exp), ss)
                marks[("a", j)] = ss.n

                def f_sg(s):
                    ga = gated_t[:, :, 1:TT + 1]
                    s.activation(ga, a_t[:, :, 1:TT + 1], AF.Square)
                    return s.activation(ga, ga, AF.Sqrt, scale=-1.0,
                                        bias=1.0)
                w = []
                if j > 0:
                    w.append((sv, marks[("hscans", j - 1)]))
                emit(S, w, f_sg, ss)
                marks[("sg", j)] = ss.n

                def f_gmul(v):
                    for c in range(KC):
                        ga = gated_t[:, c, 1:TT + 1]
                        v.tensor_mul(ga, ga, it_bf[:, c, :])
                        v.tensor_mul(ga, ga, xrg_bf[:, c, :])
                    return v.engine_nop()
                w = [(ss, marks[("sg", j)]), (ss, marks[("it", j)])]
                emit(V, w, f_gmul, sv)
                marks[("gmul", j)] = sv.n

                marks[("hs_base", j)] = sv.n

                def f_hloop(v):
                    for c in range(KC):
                        last = v.tensor_tensor_scan(
                            h_c[:], a_t[:, c, 0:TT + 1],
                            gated_t[:, c, 0:TT + 1], 0.0,
                            ALU.mult, ALU.add)
                        last.then_inc(sv.h, 1)
                        sv.n += 1
                        v.wait_ge(sv.h, sv.n)
                        v.tensor_copy(gated_t[:, c, 0:1],
                                      h_c[:, TT:TT + 1])
                        if j == NT - 1:
                            v.tensor_copy(carry[:, c:c + 1],
                                          h_c[:, TT:TT + 1])
                        last = v.tensor_mul(la_t[:, c, :],
                                            h_c[:, 1:TT + 1],
                                            xgel_t[:, c, :])
                        last.then_inc(sv.h, 1)
                        sv.n += 1
                    return None
                w = [(ss, marks[("xgel", j)])]
                if j > 0:
                    w.append((dgp, marks[("dma_u", j - 1)]))
                emit(V, w, f_hloop, None)
                marks[("hscans", j)] = sv.n

                if j == NT - 1:
                    dma(SY, cc_in, carry[:], [(sv, marks[("hscans", j)])],
                        dsy)
                    marks["cc_up"] = dsy.n

                dma(GP, sr_u[:, :, xsl], la_t[:],
                    [(sv, marks[("hscans", j)])], dgp)
                marks[("dma_u", j)] = dgp.n

                if j == NT - 1:
                    emit(GP, [(dsy, marks["cc_up"])],
                         lambda g: g.collective_compute(
                             "AllGather", ALU.bypass,
                             replica_groups=[[0, 1], [2, 3], [4, 5], [6, 7]],
                             ins=[cc_in], outs=[cc_out]), ccs)

                marks[("cp_base", j)] = sv.n

                def f_cploop(v):
                    for c in range(KC):
                        last = v.tensor_copy(zlead[:, 0:1], cpcar[:, c:c + 1])
                        last.then_inc(sv.h, 1)
                        sv.n += 1
                        v.wait_ge(sv.h, sv.n)
                        last = v.tensor_tensor_scan(
                            cph[:], a_t[:, c, 0:TT + 1],
                            zlead[:], 0.0, ALU.mult, ALU.add)
                        last.then_inc(sv.h, 1)
                        sv.n += 1
                        v.wait_ge(sv.h, sv.n)
                        v.tensor_copy(cpcar[:, c:c + 1],
                                      cph[:, TT:TT + 1])
                        last = v.tensor_mul(it_bf[:, c, :],
                                            cph[:, 1:TT + 1],
                                            xgel_t[:, c, :])
                        last.then_inc(sv.h, 1)
                        sv.n += 1
                    return None
                w = []
                if j > 0:
                    w.append((dgp, marks[("dma_v", j - 1)]))
                emit(V, w, f_cploop, None)
                marks[("scans", j)] = sv.n

                dma(GP, sr_v[:, :, xsl], it_bf[:],
                    [(sv, marks[("scans", j)])], dgp)
                marks[("dma_v", j)] = dgp.n

                if j == NT - 1:
                    dma(SY, hraw[:], cc_out[0], [(ccs, 1)], dsy)
                    marks["hraw_ld"] = dsy.n

            fA_load(0)
            fA_rows(0)
            fA_bc(0)
            for j in range(NT):
                if j == 1:
                    for q in range(4):
                        emit(GP, [],
                             lambda g, q=q: g.dma_start(
                                 out=fw1q[:, :, q * 1024:(q + 1) * 1024],
                                 in_=fw1_r[:, :, q * 1024:(q + 1) * 1024]),
                             dgp, 16)
                    marks["w1pre"] = dgp.n
                midA1(j)
                if j == NT - 1:
                    for nm, amark in [("wlin2", ("linmm", j)),
                                      ("w1a", ("xrgmm", j))]:
                        emit(GP, [(st_, marks[amark])],
                             lambda g, nm=nm: g.dma_start(out=wB[nm][:],
                                                          in_=wr_in[nm]),
                             dgp, 16)
                        marks[("wB", nm)] = dgp.n
                if j + 1 < NT:
                    fA_load(j + 1)
                    fA_rows(j + 1)
                midA2(j)
                if j == NT - 1:
                    emit(GP, [(st_, marks[("rtmm", j)])],
                         lambda g: g.dma_start(out=wB["w2a"][:],
                                               in_=wr_in["w2a"]), dgp, 16)
                    marks[("wB", "w2a")] = dgp.n
                if j + 1 < NT:
                    fA_bc(j + 1)
                midA3(j)
                if j == NT - 1:
                    emit(GP, [(st_, marks[("itmm", j)])],
                         lambda g: g.dma_start(out=wB["w3a"][:],
                                               in_=wr_in["w3a"]), dgp, 16)
                    marks[("wB", "w3a")] = dgp.n
                tailA(j)

            a_end = [(sv, sv.n), (ss, ss.n), (st_, st_.n), (gc, gc.n),
                     (dgp, dgp.n), (dsy, dsy.n), (dsc, dsc.n)]
            pha.close()

            phb = wsc.enter_context(ExitStack())
            psB1 = phb.enter_context(nc.psum_tensor("psB1", [128, TT], F32))
            psM2 = phb.enter_context(nc.psum_tensor("psM2", [128, TT], F32))
            psM3 = phb.enter_context(nc.psum_tensor("psM3", [128, TT], F32))
            psL = [psM2, psM3]
            uB = sb("uB", [128, KC, TT], BF16, phb)
            vB = sb("vB", [128, KC, TT], BF16, phb)
            ln1B = sb("ln1B", [128, KC, TT], BF16, phb)
            x0B = sb("x0B", [128, KC, TT], F32, phb)
            xghB = sb("xghB", [128, KC, TT], BF16, phb)
            res2 = sb("res2", [128, KC, TT], F32, phb)
            xmB = sb("xmB", [128, KC, TT], BF16, phb)
            outsB = sb("outsB", [128, KC, TT], BF16, phb)
            gateB = sb("gateB", [128, KC, TT], BF16, phb)
            combB = sb("combB", [128, KC, TT], BF16, phb)
            tvB = sb("tvB", [128, TT], F32, phb)
            x1B = [sb(f"x1B{i}", [128, TT], F32, phb) for i in range(4)]
            x1bf = [sb(f"x1bf{i}", [128, TT], BF16, phb) for i in range(4)]
            x1sq = [sb(f"x1sq{i}", [128, TT], BF16, phb) for i in range(4)]

            def frontB(j):
                xsl = slice(j * TT, (j + 1) * TT)
                w = [(dgp, marks[("dma_u", j)])]
                w += ([(st_, marks[("statsA", NT - 1)]),
                       (sv, marks[("ln1", NT - 1)])] if j == 0 else
                      [(st_, marks[("lin2mm", j - 1)])])
                dma(SY, uB[:], sr_u[:, :, xsl], w, dsy)
                marks[("ld_uB", j)] = dsy.n
                w = [(dgp, marks[("dma_v", j)])]
                w += ([(st_, marks[("statsA", NT - 1)]),
                       (sv, marks[("ln1", NT - 1)])] if j == 0 else
                      [(st_, marks[("lin2mm", j - 1)])])
                dma(SC_Q, vB[:], sr_v[:, :, xsl], w, dsc)
                marks[("ld_vB", j)] = dsc.n
                w = [(dsy, marks[("dma_ln1", j)])]
                w += ([(ss, marks[("xgel", NT - 1)])] if j == 0 else
                      [(sv, marks[("res2", j - 1)])])
                dma(SY, ln1B[:], sr_ln1[:, :, xsl], w, dsy)
                marks[("ld_ln1B", j)] = dsy.n
                def f_xgh(v):
                    for c in range(KC):
                        v.scalar_tensor_tensor(xghB[:, c, :], vB[:, c, :],
                                               hm[:, c:c + 1], uB[:, c, :],
                                               ALU.mult, ALU.add)
                    return v.engine_nop()
                w = [(dsy, marks[("ld_uB", j)]), (dsc, marks[("ld_vB", j)]),
                     (sv, marks["hm"])]
                if j > 0:
                    w += [(st_, marks[("lin2mm", j - 1)]),
                          (st_, marks[("statsB", j - 1)])]
                emit(V, w, f_xgh, sv)
                marks[("xgh", j)] = sv.n

            def loadB_x0(j):
                xsl = slice(j * TT, (j + 1) * TT)
                w = ([(dsy, marks[("dma_ln1", NT - 1)]),
                      (sv, marks[("scans", NT - 1)])] if j == 0 else
                     [(sv, marks[("x1", j - 1)])])
                dma(SC_Q, x0B[:], xr_in[:, :, xsl], w, dsc)
                marks[("ld_x0B", j)] = dsc.n

            def mainB(j):
                marks[("res2_base", j)] = sv.n
                w = [(sv, marks[("xgh", j)]), (dgp, marks[("wB", "wlin2")])]
                if j > 0:
                    w.append((sv, marks[("res2", j - 1)]))
                emit(T, w, mm8(wB["wlin2"], xghB, ("res2_base", j), 1, sv,
                               psL), None)
                marks[("lin2mm", j)] = st_.n

                def f_res2(v):
                    base = marks[("lin2mm", j)] - KC
                    for odc in range(KC):
                        v.wait_ge(st_.h, base + odc + 1)
                        last = v.scalar_tensor_tensor(
                            res2[:, odc, :], psL[odc % 2][:],
                            vec["blin2"][:, odc:odc + 1], ln1B[:, odc, :],
                            ALU.add, ALU.add)
                        last.then_inc(sv.h, 1)
                        sv.n += 1
                    return None
                w = [(dsy, marks[("ld_ln1B", j)])]
                if j > 0:
                    w.append((sv, marks[("xm", j - 1)]))
                else:
                    w.append((dgp, marks[("dma_u", NT - 1)]))
                emit(V, w, f_res2, None)
                marks[("res2", j)] = sv.n

                def f_sqB(s):
                    base = marks[("res2", j)] - KC
                    for odc in range(KC):
                        s.wait_ge(sv.h, base + odc + 1)
                        last = s.activation(xghB[:, odc, :], res2[:, odc, :],
                                            AF.Square)
                        last.then_inc(ss.h, 1)
                        ss.n += 1
                    return None
                emit(S, [(st_, marks[("lin2mm", j)])], f_sqB, None)
                marks[("sqB", j)] = ss.n

                def f_statsB(t):
                    base = marks[("sqB", j)] - KC
                    last = None
                    for c in range(KC):
                        t.wait_ge(ss.h, base + c + 1)
                        last = t.matmul(psS2[0:1, :], ones_bf[:, 0:1],
                                        xghB[:, c, :], start=(c == 0),
                                        stop=(c == KC - 1))
                    return last
                w = []
                if j > 0:
                    w.append((ss, marks[("crows", j - 1)]))
                emit(T, w, f_statsB, st_)
                marks[("statsB", j)] = st_.n

                emit(S, [(st_, marks[("statsB", j)])],
                     lambda s: s.activation(row(0), psS2[0:1, :], AF.Copy),
                     ss)
                marks[("rowsB1", j)] = ss.n

                def f_rB2(v):
                    v.tensor_scalar_max(row(1), row(0), 1e-24)
                    return v.reciprocal(row(2), row(1))
                emit(V, [(ss, marks[("rowsB1", j)])], f_rB2, sv)
                marks[("rowsB2", j)] = sv.n

                emit(S, [(sv, marks[("rowsB2", j)])],
                     lambda s: s.activation(row(2), row(2), AF.Sqrt,
                                            scale=1024.0), ss)
                marks[("rowsB3", j)] = ss.n

                emit(V, [(ss, marks[("rowsB3", j)])],
                     lambda v: v.tensor_copy(rbf(0), row(2)), sv)
                marks[("rowsB4", j)] = sv.n

                w = [(sv, marks[("rowsB4", j)])]
                if j > 0:
                    w.append((sv, marks[("xm", j - 1)]))
                emit(T, w,
                     lambda t: t.matmul(psB1[:, :], ones_fb[0:1, :], rbf(0),
                                        start=True, stop=True), st_)
                marks[("bcastB", j)] = st_.n

                def f_xm(v):
                    for c in range(KC):
                        v.scalar_tensor_tensor(xmB[:, c, :], res2[:, c, :],
                                               vec["g2"][:, c:c + 1],
                                               psB1[:, :],
                                               ALU.mult, ALU.mult)
                    return v.engine_nop()
                w = [(st_, marks[("bcastB", j)])]
                if j > 0:
                    w.append((st_, marks[("w2mm", j - 1)]))
                emit(V, w, f_xm, sv)
                marks[("xm", j)] = sv.n

                marks[("outs_base", j)] = ss.n
                w = [(sv, marks[("xm", j)]), (dgp, marks[("wB", "w1a")])]
                if j > 0:
                    w.append((sv, marks[("x1", j - 1)]))
                emit(T, w, mm8(wB["w1a"], xmB, ("outs_base", j), 1, ss, psM),
                     None)
                marks[("w1mm", j)] = st_.n

                def f_outs(s):
                    base = marks[("w1mm", j)] - KC
                    for odc in range(KC):
                        s.wait_ge(st_.h, base + odc + 1)
                        last = s.activation(outsB[:, odc, :],
                                            psM[odc % 2][:], AF.Sigmoid,
                                            bias=vec["b1"][:, odc:odc + 1])
                        last.then_inc(ss.h, 1)
                        ss.n += 1
                    return None
                emit(S, [], f_outs, None)
                marks[("outs", j)] = ss.n

                w = []
                if j > 0:
                    w.append((sv, marks[("comb", j - 1)]))
                else:
                    w.append((dgp, marks[("dma_v", NT - 1)]))
                emit(S, w,
                     lambda s: s.activation(gateB[:], outsB[:], AF.Gelu), ss)
                marks[("gate", j)] = ss.n

            def tailB(j):
                xsl = slice(j * TT, (j + 1) * TT)
                marks[("comb_base", j)] = sv.n
                emit(T, [(ss, marks[("outs", j)]),
                         (dgp, marks[("wB", "w2a")])],
                     mm8(wB["w2a"], xmB, ("comb_base", j), 1, sv, psM), None)
                marks[("w2mm", j)] = st_.n

                def f_comb(v):
                    base = marks[("w2mm", j)] - KC
                    for odc in range(KC):
                        v.wait_ge(st_.h, base + odc + 1)
                        last = v.scalar_tensor_tensor(
                            combB[:, odc, :], psM[odc % 2][:],
                            vec["b2"][:, odc:odc + 1], gateB[:, odc, :],
                            ALU.add, ALU.mult)
                        last.then_inc(sv.h, 1)
                        sv.n += 1
                    return None
                emit(V, [(ss, marks[("gate", j)])], f_comb, None)
                marks[("comb", j)] = sv.n

                marks[("x1_base", j)] = sv.n
                emit(T, [(sv, marks[("comb", j)]),
                         (dgp, marks[("wB", "w3a")])],
                     mm8(wB["w3a"], combB, ("x1_base", j), 2, sv, psM), None)
                marks[("w3mm", j)] = st_.n

                marks[("x1dma_base", j)] = dsy.n
                marks[("x1bf_base", j)] = ss.n

                def f_x1(v):
                    base = marks[("w3mm", j)] - KC
                    for odc in range(KC):
                        v.wait_ge(st_.h, base + odc + 1)
                        if odc >= 4:
                            v.wait_ge(dsy.h, marks[("x1dma_base", j)] +
                                      16 * (odc - 3))
                            v.wait_ge(ss.h, marks[("x1bf_base", j)] +
                                      odc - 3)
                        v.scalar_tensor_tensor(tvB[:], psM[odc % 2][:],
                                               vec["b3"][:, odc:odc + 1],
                                               res2[:, odc, :],
                                               ALU.add, ALU.add)
                        v.tensor_add(x1B[odc % 4][:], tvB[:],
                                     x0B[:, odc, :])
                        last = v.tensor_mul(x1sq[odc % 4][:],
                                            x1B[odc % 4][:],
                                            x1B[odc % 4][:])
                        last.then_inc(sv.h, 2)
                        sv.n += 2
                    return None
                w = [(dsc, marks[("ld_x0B", j)])]
                if j > 0:
                    w.append((st_, marks[("x1stats", j - 1)]))
                emit(V, w, f_x1, None)
                marks[("x1", j)] = sv.n

                def f_x1bf(s):
                    base = marks[("x1", j)] - 2 * KC
                    for odc in range(KC):
                        s.wait_ge(sv.h, base + 2 * (odc + 1))
                        last = s.activation(x1bf[odc % 4][:],
                                            x1B[odc % 4][:], AF.Copy)
                        last.then_inc(ss.h, 1)
                        ss.n += 1
                    return None
                w = []
                if j > 0:
                    w.append((st_, marks[("x1stats", j - 1)]))
                emit(S, w, f_x1bf, None)
                marks[("x1bf", j)] = ss.n

                def f_x1stats(t):
                    bfb = marks[("x1bf", j)] - KC
                    last = None
                    for odc in range(KC):
                        t.wait_ge(ss.h, bfb + odc + 1)
                        t.matmul(psS1[0:1, :], ones_bf[:, 0:1],
                                 x1bf[odc % 4][:], start=(odc == 0),
                                 stop=(odc == KC - 1))
                        last = t.matmul(psS2[0:1, :], ones_bf[:, 0:1],
                                        x1sq[odc % 4][:], start=(odc == 0),
                                        stop=(odc == KC - 1))
                    return last
                w = [(ss, marks[("rowsB1", j)])]
                emit(T, w, f_x1stats, st_)
                marks[("x1stats", j)] = st_.n

                def f_crows(s):
                    s.activation(crows[0:1, 2 * j * TT:(2 * j + 1) * TT],
                                 psS1[0:1, :], AF.Copy, scale=1.0 / 1024.0)
                    return s.activation(
                        crows[0:1, (2 * j + 1) * TT:(2 * j + 2) * TT],
                        psS2[0:1, :], AF.Copy, scale=1.0 / 1024.0)
                emit(S, [(st_, marks[("x1stats", j)])], f_crows, ss)
                marks[("crows", j)] = ss.n

                def f_x1dma(g):
                    base = marks[("x1", j)] - 2 * KC
                    for odc in range(KC):
                        g.wait_ge(sv.h, base + 2 * (odc + 1))
                        last = g.dma_start(out=sr_x1[:, odc, xsl],
                                           in_=x1B[odc % 4][:])
                        last.then_inc(dsy.h, 16)
                        dsy.n += 16
                    return None
                emit(SY, [], f_x1dma, None)
                marks[("x1dma", j)] = dsy.n

            emit(V, [(dsy, marks["hraw_ld"])],
                 lambda v: v.tensor_mul(hm[:], hraw[:], vec["mask"][:]), sv)
            marks["hm"] = sv.n
            frontB(0)
            loadB_x0(0)
            for j in range(NT):
                mainB(j)
                if j + 1 < NT:
                    frontB(j + 1)
                tailB(j)
                if j == 0:
                    rowsC_pre(0)
                if j + 1 < NT:
                    loadB_x0(j + 1)

            b_end = [(sv, sv.n), (ss, ss.n), (st_, st_.n), (gc, gc.n),
                     (dgp, dgp.n), (dsy, dsy.n), (dsc, dsc.n)]
            phb.close()

        with ExitStack() as phc:
            psB2 = phc.enter_context(nc.psum_tensor("psB2", [128, 2 * TT],
                                                    F32))
            psM2c = phc.enter_context(nc.psum_tensor("psM2c", [128, TT],
                                                     F32))
            psM3c = phc.enter_context(nc.psum_tensor("psM3c", [128, TT],
                                                     F32))
            psW = [psM2c, psM3c]
            x1C = [sb(f"x1C{i}", [128, KC, TT], F32, phc) for i in range(2)]
            w1sb = sb("w1sb", [128, KC, H], BF16, phc)
            w2sb = sb("w2sb", [128, HC, D], FP8, phc)
            region_w = [(sv, marks[("x1", NT - 1)]),
                        (st_, marks[("w3mm", NT - 1)]),
                        (ss, marks[("gate", NT - 1)]),
                        (dgp, marks["w1pre"])]
            dma(SY, w1sb[:, :, 0:1024], fw1q[:, :, 0:1024], region_w, dsy)
            marks[("w1q", 0)] = dsy.n
            dma(SY, w2sb[:], fw2q,
                [(st_, marks[("x1stats", NT - 1)]), (dgp, marks["w2pre"])],
                dsy)
            w2_done = dsy.n
            for q in (1, 3):
                dma(SY, w1sb[:, :, q * 1024:(q + 1) * 1024],
                    fw1q[:, :, q * 1024:(q + 1) * 1024], [], dsy)
                marks[("w1q", q)] = dsy.n

            ln2C = [sb(f"ln2C{i}", [128, KC, TT], BF16, phc)
                    for i in range(2)]
            hqC = sb("hqC", [128, HC, TT], FP8, phc)
            o16 = [sb(f"o16_{i}", [128, TT], FP16, phc) for i in range(2)]
            xoC = [sb(f"xoC{i}", [128, TT], F32, phc) for i in range(2)]
            tmpC = sb("tmpC", [128, TT], F32, phc)

            def frontC(j):
                xsl = slice(j * TT, (j + 1) * TT)
                x1t = x1C[j % 2]
                w = [(dsy, marks[("x1dma", j)])]
                if j == 0:
                    w.append((sv, marks[("xgh", NT - 1)]))
                if j == 1:
                    w.append((sv, marks[("x1", NT - 1)]))
                if j > 1:
                    w.append((sv, marks[("xoutV", j - 2)]))
                dma(SC_Q, x1t[:], sr_x1[:, :, xsl], w, dsc)
                marks[("ld_x1C", j)] = dsc.n

                if j > 0:
                    rowsC_pre(j)

                def f_bcastC(t):
                    t.matmul(psB2[:, 0:TT], ones_fb[0:1, :], rbf(1),
                             start=True, stop=True)
                    return t.matmul(psB2[:, TT:2 * TT], ones_fb[0:1, :],
                                    rbf(2), start=True, stop=True)
                w = [(sv, marks[("rowsC3", j)])]
                if j > 0:
                    w.append((sv, marks[("ln2", j - 1)]))
                emit(T, w, f_bcastC, st_)
                marks[("bcastC", j)] = st_.n

                def f_ln2(v):
                    for c in range(KC):
                        v.tensor_mul(tmpC[:], x1t[:, c, :], psB2[:, 0:TT])
                        v.tensor_sub(ln2C[j % 2][:, c, :], tmpC[:],
                                     psB2[:, TT:2 * TT])
                    return v.engine_nop()
                w = [(st_, marks[("bcastC", j)]),
                     (dsc, marks[("ld_x1C", j)])]
                if j > 1:
                    w.append((st_, marks[("w1mmC", j - 2)]))
                emit(V, w, f_ln2, sv)
                marks[("ln2", j)] = sv.n

            def mainC(j):
                marks[("hq_base", j)] = ss.n

                def f_w1(t):
                    for hc in range(HC):
                        if hc >= 2:
                            t.wait_ge(ss.h, marks[("hq_base", j)] + hc - 1)
                        if hc == 8:
                            t.wait_ge(dsy.h, marks[("w1q", 1)])
                        if hc == 16:
                            t.wait_ge(dsc.h, marks[("w1q", 2)])
                        if hc == 24:
                            t.wait_ge(dsy.h, marks[("w1q", 3)])
                        last = None
                        for kc in range(KC):
                            last = t.matmul(
                                psM[hc % 2][:],
                                w1sb[:, kc, hc * 128:(hc + 1) * 128],
                                ln2C[j % 2][:, kc, :],
                                start=(kc == 0), stop=(kc == KC - 1))
                        last.then_inc(st_.h, 1)
                        st_.n += 1
                    return None
                w = [(sv, marks[("ln2", j)]), (dsy, marks[("w1q", 0)])]
                if j > 0:
                    w.append((ss, marks[("hq", j - 1)]))
                emit(T, w, f_w1, None)
                marks[("w1mmC", j)] = st_.n

                def f_hq(s):
                    base = marks[("w1mmC", j)] - HC
                    for hc in range(HC):
                        s.wait_ge(st_.h, base + hc + 1)
                        last = s.activation(hqC[:, hc, :], psM[hc % 2][:],
                                            AF.Relu,
                                            bias=vec["fb1"][:, hc:hc + 1])
                        last.then_inc(ss.h, 1)
                        ss.n += 1
                    return None
                w = []
                if j > 0:
                    w.append((st_, marks[("w2mmC", j - 1)]))
                emit(S, w, f_hq, None)
                marks[("hq", j)] = ss.n

            def tailC(j):
                xsl = slice(j * TT, (j + 1) * TT)
                x1t = x1C[j % 2]
                marks[("o16_base", j)] = ss.n

                def f_w2(t):
                    for odc in range(KC):
                        if odc >= 2:
                            t.wait_ge(ss.h, marks[("o16_base", j)] + odc - 1)
                        last = None
                        for ck in range(HC // 2):
                            last = t.matmul(
                                psW[odc % 2][:],
                                w2sb[:, 2 * ck:2 * ck + 2,
                                     odc * 128:(odc + 1) * 128],
                                hqC[:, 2 * ck:2 * ck + 2, :],
                                start=(ck == 0), stop=(ck == HC // 2 - 1),
                                perf_mode=DR)
                        last.then_inc(st_.h, 1)
                        st_.n += 1
                    return None
                w = [(ss, marks[("hq", j)]), (dsy, w2_done)]
                if j > 0:
                    w.append((ss, marks[("o16", j - 1)]))
                emit(T, w, f_w2, None)
                marks[("w2mmC", j)] = st_.n

                marks[("xoutV_base", j)] = sv.n

                def f_o16(s):
                    base = marks[("w2mmC", j)] - KC
                    for odc in range(KC):
                        s.wait_ge(st_.h, base + odc + 1)
                        if odc >= 2:
                            s.wait_ge(sv.h, marks[("xoutV_base", j)] +
                                      odc - 1)
                        last = s.activation(o16[odc % 2][:], psW[odc % 2][:],
                                            AF.Identity,
                                            bias=bq32[:, odc:odc + 1])
                        last.then_inc(ss.h, 1)
                        ss.n += 1
                    return None
                emit(S, [], f_o16, None)
                marks[("o16", j)] = ss.n

                marks[("xodma_base", j)] = dsy.n

                def f_xout(v):
                    base = marks[("o16", j)] - KC
                    for odc in range(KC):
                        v.wait_ge(ss.h, base + odc + 1)
                        if odc >= 2:
                            v.wait_ge(dsy.h, marks[("xodma_base", j)] +
                                      16 * (odc - 1))
                        last = v.tensor_add(xoC[odc % 2][:], o16[odc % 2][:],
                                            x1t[:, odc, :])
                        last.then_inc(sv.h, 1)
                        sv.n += 1
                    return None
                emit(V, [], f_xout, None)
                marks[("xoutV", j)] = sv.n

                def f_xodma(sync):
                    base = marks[("xoutV", j)] - KC
                    for odc in range(KC):
                        sync.wait_ge(sv.h, base + odc + 1)
                        last = sync.dma_start(out=xr_out[:, odc, xsl],
                                              in_=xoC[odc % 2][:])
                        last.then_inc(dsy.h, 16)
                        dsy.n += 16
                    return None
                emit(SY, [], f_xodma, None)
                marks[("xodma", j)] = dsy.n

            frontC(0)
            frontC(1)
            dma(SC_Q, w1sb[:, :, 2 * 1024:3 * 1024],
                fw1q[:, :, 2 * 1024:3 * 1024],
                [(st_, marks[("w3mm", NT - 1)]),
                 (ss, marks[("gate", NT - 1)]), (dgp, marks["w1pre"])], dsc)
            marks[("w1q", 2)] = dsc.n
            for j in range(NT):
                mainC(j)
                tailC(j)
                if j + 2 < NT:
                    frontC(j + 2)

            emit(SY, [(dsy, dsy.n)], lambda e: None, None)

    return nc


_CACHE = {}


def _get_prog():
    if "nc" not in _CACHE:
        _CACHE["nc"] = _build()
    return _CACHE["nc"]


def kernel(**inputs):
    inp = {k: np.asarray(v) for k, v in inputs.items()}

    def vstage(v, kc):
        v = np.asarray(v, np.float32)
        return np.ascontiguousarray(v.reshape(kc, 128).T)

    def wt(name):
        return np.ascontiguousarray(np.asarray(inp[name], np.float32).T)

    shared = {
        "wlin": wt("L0_attn_W_lin"), "wconv": wt("L0_attn_W_conv"),
        "wa": wt("L0_attn_Wa"), "wx": wt("L0_attn_Wx"),
        "wlin2": wt("L0_attn_W_lin2"), "w1a": wt("L0_attn_W1"),
        "w2a": wt("L0_attn_W2"), "w3a": wt("L0_attn_W3"),
        "fw1": wt("L0_ffn_w1"), "fw2": wt("L0_ffn_w2"),
        "vg1": vstage(inp["L0_attn_g1"], KC),
        "vg2": vstage(inp["L0_attn_g2"], KC),
        "vblin": vstage(inp["L0_attn_b_lin"], KC),
        "vbconv": vstage(inp["L0_attn_b_conv"], KC),
        "vba": vstage(inp["L0_attn_ba"], KC),
        "vbx": vstage(inp["L0_attn_bx"], KC),
        "vblin2": vstage(inp["L0_attn_b_lin2"], KC),
        "vb1": vstage(inp["L0_attn_b1"], KC),
        "vb2": vstage(inp["L0_attn_b2"], KC),
        "vb3": vstage(inp["L0_attn_b3"], KC),
        "vlam": vstage(inp["L0_attn_Lam"], KC),
        "vfb2": vstage(inp["L0_ffn_b2"], KC),
        "vfb1": vstage(inp["L0_ffn_b1"], HC),
    }

    x = np.asarray(inp["L0_attn_x"], np.float32)
    in_maps = []
    for c in range(NCORE):
        b, half = c // 2, c % 2
        t0 = half * TPC
        m = dict(shared)
        m["x"] = np.ascontiguousarray(x[b, t0:t0 + TPC, :].T)
        m["vmask"] = np.full((128, KC), float(half), np.float32)
        in_maps.append(m)

    nc = _get_prog()
    res = run_bass_kernel_spmd(nc, in_maps, core_ids=list(range(NCORE)))

    out = np.empty((4, 4096, D), np.float32)
    for c in range(NCORE):
        b, half = c // 2, c % 2
        t0 = half * TPC
        out[b, t0:t0 + TPC, :] = res.results[c]["xout"].T
    return out
